# version 20
# baseline (speedup 1.0000x reference)
import numpy as np
import ml_dtypes
import concourse.bass as bass
import concourse.mybir as mybir
from concourse.bass_utils import run_bass_kernel_spmd

F32 = mybir.dt.float32
I32 = mybir.dt.int32
BF16 = mybir.dt.bfloat16
FP8 = mybir.dt.float8e5
AF = mybir.ActivationFunctionType
ALU = mybir.AluOpType

T, D, E, NST, RDT = 4096, 1024, 2048, 16, 64
D4 = 4 * D
Tc = 256
DC = E // 128
DD = D // 128
MC1 = D4 // 128
NH = NST // 2


def build_core(nch=T // Tc, dbg=False, paranoid=False):
    nc = bass.Bass("TRN2", target_bir_lowering=False, detect_race_conditions=False)

    x_in = nc.dram_tensor("x_in", [T, D], F32, kind="ExternalInput").ap()
    w_inT = nc.dram_tensor("w_inT", [2 * E // 128, 128, DD, 128], F32,
                           kind="ExternalInput").ap()
    w_xT = nc.dram_tensor("w_xT", [E, 96], F32, kind="ExternalInput").ap()
    w_dtT = nc.dram_tensor("w_dtT", [RDT, E], F32, kind="ExternalInput").ap()
    w_outT = nc.dram_tensor("w_outT", [DD, 128, DC, 128], F32,
                            kind="ExternalInput").ap()
    w1T = nc.dram_tensor("w1T", [MC1, 128, DD, 128], F32,
                         kind="ExternalInput").ap()
    w2T = nc.dram_tensor("w2T", [DD, 128, MC1, 128], F32,
                         kind="ExternalInput").ap()
    bconv_in = nc.dram_tensor("bconv_in", [128, DC], F32, kind="ExternalInput").ap()
    wconv_in = nc.dram_tensor("wconv_in", [128, DC, 4], F32, kind="ExternalInput").ap()
    bdt_in = nc.dram_tensor("bdt_in", [128, DC], F32, kind="ExternalInput").ap()
    alog_in = nc.dram_tensor("alog_in", [128, DC * NST], F32, kind="ExternalInput").ap()
    dvec_in = nc.dram_tensor("dvec_in", [128, DC], F32, kind="ExternalInput").ap()
    rmsw_in = nc.dram_tensor("rmsw_in", [128, DC], F32, kind="ExternalInput").ap()
    b1_in = nc.dram_tensor("b1_in", [128, MC1], F32, kind="ExternalInput").ap()
    b2_in = nc.dram_tensor("b2_in", [128, DD], F32, kind="ExternalInput").ap()
    ident_in = nc.dram_tensor("ident_in", [128, 128], F32, kind="ExternalInput").ap()
    onesc_in = nc.dram_tensor("onesc_in", [1, 128], F32, kind="ExternalInput").ap()
    onesr_in = nc.dram_tensor("onesr_in", [128, 1], F32, kind="ExternalInput").ap()
    mask_in = nc.dram_tensor("mask_in", [D, T], BF16, kind="ExternalInput").ap()
    y_out = nc.dram_tensor("y_out", [T, D], F32, kind="ExternalOutput").ap()

    winb_s = nc.dram_tensor("winb_s", [2 * E // 128, 128, DD, 128], BF16,
                            kind="ExternalOutput").ap()
    woutb_s = nc.dram_tensor("woutb_s", [DD, 128, DC, 128], BF16,
                             kind="ExternalOutput").ap()
    w1q_s = nc.dram_tensor("w1q_s", [MC1, 128, DD, 128], FP8,
                           kind="ExternalOutput").ap()
    w2q_s = nc.dram_tensor("w2q_s", [DD, 128, MC1, 128], FP8,
                           kind="ExternalOutput").ap()
    b_sc = nc.dram_tensor("b_sc", [NST, T], F32, kind="ExternalOutput").ap()
    c_sc = nc.dram_tensor("c_sc", [NST, T], F32, kind="ExternalOutput").ap()

    sbuf = nc.alloc_sbuf_tensor
    x_fm = sbuf("x_fm", [128, DD, Tc], F32)
    xrow = sbuf("xrow", [128, DD, 128], F32)
    xn_bf = sbuf("xn_bf", [128, DD, Tc], BF16)
    xnq = sbuf("xnq", [128, DD, Tc], FP8)
    xi = sbuf("xi", [128, DC, Tc + 3], F32)
    xc = sbuf("xc", [128, DC, Tc], F32)
    xcb = sbuf("xcb", [128, DC, Tc], BF16)
    zs = sbuf("zs", [128, DC, Tc], F32)
    dtt = sbuf("dtt", [128, DC, Tc], F32)
    sA = sbuf("sA", [128, NH, Tc], F32)
    sB = sbuf("sB", [128, NH, Tc], F32)
    sB2 = sbuf("sB2", [128, NH, Tc], F32)
    yt = sbuf("yt", [128, DC, Tc], F32)
    brep = sbuf("brep", [128, NH, Tc], F32)
    crep = sbuf("crep", [128, NH, Tc], F32)
    vtmp = sbuf("vtmp", [128, DC, Tc], F32)
    mkb = sbuf("mkb", [128, DD, Tc], BF16)
    hstate = sbuf("hstate", [128, DC, NST], F32)
    halo = sbuf("halo", [128, DC, 3], F32)
    dtlow = sbuf("dtlow", [RDT, Tc], BF16)
    bsb = dtt[0:NST, 0, 0:Tc]
    csb = dtt[0:NST, 1, 0:Tc]
    stt = xrow[:].rearrange("p a b -> p (a b)")[0:1, 0:3 * Tc]
    stt1 = stt[:, 0:Tc]
    stt2 = stt[:, Tc:2 * Tc]
    stt3 = stt[:, 2 * Tc:3 * Tc]
    ci1 = sbuf("ci1", [128, 1024], I32)
    ci2 = sbuf("ci2", [128, 1024], I32)
    hq = sbuf("hq", [128, MC1, Tc], FP8)
    q_eb = ci1[:, 0:Tc]
    q_bs = ci1[:, Tc:2 * Tc]
    q_br = ci1[:, 2 * Tc:3 * Tc]
    q_fl = ci1[:, 3 * Tc:4 * Tc]

    wb_in = sbuf("wb_in", [128, 2, DD, 128], BF16)
    wb_out = sbuf("wb_out", [128, DC, 128], BF16)
    wb_1 = sbuf("wb_1", [128, 2, DD, 128], FP8)
    wb_2 = sbuf("wb_2", [128, MC1, 128], FP8)
    wx_sb = sbuf("wx_sb", [128, DC, 96], BF16)
    wdt_sb = sbuf("wdt_sb", [RDT, DC, 128], BF16)
    bconv = sbuf("bconv", [128, DC], F32)
    wconv = sbuf("wconv", [128, DC, 4], F32)
    bdt = sbuf("bdt", [128, DC], F32)
    asb = sbuf("asb", [128, DC * NST], F32)
    dvec = sbuf("dvec", [128, DC], F32)
    rmsw = sbuf("rmsw", [128, DC], F32)
    b1q = sbuf("b1q", [128, MC1], F32)
    b2q = sbuf("b2q", [128, DD], F32)
    ident = sbuf("ident", [128, 128], F32)
    onesc = sbuf("onesc", [1, 128], F32)
    onesr = sbuf("onesr", [128, 1], F32)
    cbias = sbuf("cbias", [128, 3], F32)

    q_y1 = vtmp[:, 0, :]
    q_m = vtmp[:, 1, :]
    q_st = vtmp[:, 2, :]
    q_t2 = vtmp[:, 3, :]
    qb_y1 = vtmp[:, 0:4, :].rearrange("p a b -> p (a b)")
    qb_m = vtmp[:, 4:8, :].rearrange("p a b -> p (a b)")
    qb_st = vtmp[:, 8:12, :].rearrange("p a b -> p (a b)")
    qb_t2 = vtmp[:, 12:16, :].rearrange("p a b -> p (a b)")
    qb_i1 = ci1[:, 0:1024]
    qb_i2 = ci2[:, 0:1024]
    def flat(t):
        return t[:].rearrange("p a b -> p (a b)")
    vt_f = flat(vtmp)
    yt_f = flat(yt)
    xc_f = flat(xc)

    xcb_f = flat(xcb)


    psum = nc.alloc_psum_tensor
    psA0 = psum("psA0", [128, Tc], F32)
    psA1 = psum("psA1", [128, Tc], F32)
    psT = psum("psT", [128, Tc], F32)
    psS1 = psum("psS1", [1, Tc], F32)
    psS2 = psum("psS2", [1, Tc], F32)
    psB1 = psum("psB1", [128, Tc], F32)
    psB2 = psum("psB2", [128, Tc], F32)

    sem_names = ["sS", "sG", "sP", "sV", "sA", "wA", "wB", "wC", "wD",
                 "wE", "wF", "wG", "wH", "wM", "wI", "wJ"]
    sems = {n: nc.alloc_semaphore(n) for n in sem_names}
    cnt = {n: 0 for n in sem_names}
    sched = []
    esem = {"sync": "sS", "gpsimd": "sG", "tensor": "sP", "vector": "sV", "scalar": "sA"}

    def snap(*ss):
        return [(s, cnt[s]) for s in ss]

    def op(eng, fn, waits=(), inc=True, d=1, sem=None):
        if paranoid:
            waits = list(waits) + snap("sS", "sG", "sP", "sV", "sA")
        best = {}
        for (s, v) in waits:
            if v > best.get(s, 0):
                best[s] = v
        w = [(s, v) for s, v in best.items() if v > 0]
        s = sem if sem is not None else (esem[eng] if inc else None)
        if s:
            cnt[s] += d
        sched.append((eng, w, fn, (s, d) if s else None))

    def dma(eng, out, in_, waits=(), sem=None, **kw):
        op(eng, lambda e, o=out, i=in_, k=kw: e.dma_start(out=o, in_=i, **k),
           waits, d=16, sem=sem)

    def fq_chain(e, src, dst, eb, bs, br, t1, t2, fl):
        e.tensor_single_scalar(eb, src.bitcast(I32), 0x7F800000, ALU.bitwise_and)
        e.tensor_single_scalar(eb, eb, 113 << 23, ALU.max)
        e.tensor_single_scalar(bs, eb, 2 << 23, ALU.subtract)
        e.tensor_scalar(br, bs, -1, 254 << 23, ALU.mult, ALU.add)
        e.tensor_tensor(t1, src, br.bitcast(F32), ALU.mult)
        e.tensor_single_scalar(t1, t1, 24.0, ALU.add)
        e.tensor_single_scalar(fl, t1.bitcast(I32), ~0x7FFFF, ALU.bitwise_and)
        e.tensor_tensor(t2, t1, fl.bitcast(F32), ALU.subtract)
        e.tensor_single_scalar(dst, t2, 0.5, ALU.is_equal)
        e.scalar_tensor_tensor(t2, t2, 0.5, fl.bitcast(F32), ALU.is_gt, ALU.add)
        e.tensor_single_scalar(br, fl, 19, ALU.logical_shift_right)
        e.tensor_single_scalar(br, br, 1, ALU.bitwise_and)
        e.tensor_copy(t1, br)
        e.tensor_tensor(t1, t1, dst, ALU.mult)
        e.tensor_tensor(t2, t2, t1, ALU.add)
        return e.scalar_tensor_tensor(dst, t2, -24.0, bs.bitcast(F32), ALU.add, ALU.mult)

    for o, i in [(bconv[:], bconv_in), (wconv[:], wconv_in), (bdt[:], bdt_in),
                 (dvec[:], dvec_in), (rmsw[:], rmsw_in), (b1q[:], b1_in),
                 (b2q[:], b2_in), (ident[:], ident_in), (onesc[:], onesc_in),
                 (onesr[:], onesr_in), (asb[:], alog_in)]:
        dma("sync", o, i)
    dma("sync", vtmp[:, :, 0:96], w_xT.rearrange("(kc p) c -> p kc c", p=128))
    dma("sync", yt_f[0:RDT, 0:DC * 128], w_dtT)

    op("vector", lambda e: e.memset(cbias[:, 0:1], 1e-5))
    op("vector", lambda e: e.memset(cbias[:, 1:2], 1.0))
    op("vector", lambda e: e.memset(cbias[:, 2:3], 1e-6))
    op("scalar", lambda e: e.activation(asb[:], asb[:], AF.Exp), waits=snap("sS"))
    op("vector", lambda e: e.tensor_single_scalar(asb[:], asb[:], -1.0, ALU.mult),
       waits=snap("sA"))
    op("vector", lambda e: e.tensor_copy(wx_sb[:], vtmp[:, :, 0:96]),
       waits=snap("sS"))
    op("vector", lambda e: e.tensor_copy(flat(wdt_sb), yt_f[0:RDT, 0:DC * 128]))

    p0c = snap("sS", "sG", "sP", "sV", "sA")
    cf = vt_f
    ct1 = yt_f
    ct2 = xc_f
    cb = xcb_f
    cq = flat(hq)
    fl2 = lambda a: a.rearrange("p a b -> p (a b)")
    cvd = [cnt["sV"], cnt["sV"]]
    std = [0, 0]
    def p0_stream(n, width, src_fn, dst_fn, kind):
        for i in range(n):
            s = i % 2
            o = s * 2048
            lsem = "wA" if s == 0 else "wB"
            ssem = "wC" if s == 0 else "wD"
            dma("sync", cf[:, o:o + width], src_fn(i), waits=[("sV", cvd[s])],
                sem=lsem)
            ld = cnt[lsem]
            ob = cb if kind == "bf" else cq
            if kind == "bf":
                op("vector", lambda e, o=o, width=width: e.tensor_copy(
                    cb[:, o:o + width], cf[:, o:o + width]),
                   waits=[(lsem, ld), (ssem, std[s])])
            else:
                for q in range(width // D):
                    op("vector", lambda e, o=o, q=q: fq_chain(
                        e, cf[:, o + q * D:o + (q + 1) * D],
                        cq[:, o + q * D:o + (q + 1) * D], ci1[:, 0:D],
                        ci2[:, 0:D], ci1[:, 0:D], ct1[:, 0:D], ct2[:, 0:D],
                        ci1[:, 0:D]),
                       waits=[(lsem, ld), (ssem, std[s])] if q == 0 else [])
            cvd[s] = cnt["sV"]
            dma("gpsimd", dst_fn(i), ob[:, o:o + width], waits=[("sV", cvd[s])],
                sem=ssem)
            std[s] = cnt[ssem]
    p0_stream(2 * E // 128, D, lambda i: fl2(w_inT[i]),
              lambda i: fl2(winb_s[i]), "bf")
    p0_stream(DD, 2 * D, lambda i: fl2(w_outT[i]),
              lambda i: fl2(woutb_s[i]), "bf")
    p0_stream(MC1, D, lambda i: fl2(w1T[i]),
              lambda i: fl2(w1q_s[i]), "fq")
    p0_stream(4 * DD, D,
              lambda i: fl2(w2T[i // 4])[:, (i % 4) * D:(i % 4 + 1) * D],
              lambda i: fl2(w2q_s[i // 4])[:, (i % 4) * D:(i % 4 + 1) * D],
              "fq")
    def fq_steps(src_, dst_, eb, bs, br, t1, t2, fl):
        return [
            lambda e: e.tensor_single_scalar(eb, src_.bitcast(I32), 0x7F800000,
                                             ALU.bitwise_and),
            lambda e: e.tensor_single_scalar(eb, eb, 113 << 23, ALU.max),
            lambda e: e.tensor_single_scalar(bs, eb, 2 << 23, ALU.subtract),
            lambda e: e.tensor_scalar(br, bs, -1, 254 << 23, ALU.mult, ALU.add),
            lambda e: e.tensor_tensor(t1, src_, br.bitcast(F32), ALU.mult),
            lambda e: e.tensor_single_scalar(t1, t1, 24.0, ALU.add),
            lambda e: e.tensor_single_scalar(fl, t1.bitcast(I32), ~0x7FFFF,
                                             ALU.bitwise_and),
            lambda e: e.tensor_tensor(t2, t1, fl.bitcast(F32), ALU.subtract),
            lambda e: e.tensor_single_scalar(dst_, t2, 0.5, ALU.is_equal),
            lambda e: e.scalar_tensor_tensor(t2, t2, 0.5, fl.bitcast(F32),
                                             ALU.is_gt, ALU.add),
            lambda e: e.tensor_single_scalar(br, fl, 19, ALU.logical_shift_right),
            lambda e: e.tensor_single_scalar(br, br, 1, ALU.bitwise_and),
            lambda e: e.tensor_copy(t1, br),
            lambda e: e.tensor_tensor(t1, t1, dst_, ALU.mult),
            lambda e: e.tensor_tensor(t2, t2, t1, ALU.add),
            lambda e: e.scalar_tensor_tensor(dst_, t2, -24.0, bs.bitcast(F32),
                                             ALU.add, ALU.mult),
        ]
    first = True
    for st_fn in fq_steps(b1q[:], b1q[:], q_eb[:, 0:MC1], q_bs[:, 0:MC1],
                          q_br[:, 0:MC1], q_m[:, 0:MC1], q_t2[:, 0:MC1],
                          q_fl[:, 0:MC1]):
        op("vector", st_fn, waits=snap("sS", "sG", "sV") if first else snap("sV"))
        first = False
    for st_fn in fq_steps(b2q[:], b2q[:], q_eb[:, 0:DD], q_bs[:, 0:DD],
                          q_br[:, 0:DD], q_m[:, 0:DD], q_t2[:, 0:DD],
                          q_fl[:, 0:DD]):
        op("vector", st_fn, waits=snap("sV"))
    op("vector", lambda e: e.memset(hstate[:], 0.0))
    op("vector", lambda e: e.memset(halo[:], 0.0))
    p0s = snap("sS", "sG", "sP", "sV", "sA")
    p0w = snap("wC", "wD")
    wbin_last = [0, 0]
    wbo_last = [0, 0]
    wb1_last = [0, 0]
    wb2_last = [0, 0]
    wbo_bufs = None
    mk_cv = [0]

    for j in range(nch):
        t0 = j * Tc
        x_dram = x_in.rearrange("(tt ts p) (dd f) -> tt ts p dd f", p=128, ts=2, f=128)
        y_dram = y_out.rearrange("(tt ts p) (dd f) -> tt ts p dd f", p=128, ts=2, f=128)

        for ts in range(2):
            dma("sync", xrow[:], x_dram[j, ts],
                waits=snap("sG", "sP") + (p0c if j == 0 else []))
            x_ld = cnt["sS"]
            for dd in range(DD):
                op("tensor", lambda e, dd=dd: e.transpose(
                    psT[:, 0:128], xrow[:, dd, :], ident[:]),
                    waits=[("sS", x_ld)] + snap("sV", "sA") + (p0c if j == 0 else []))
                op("vector", lambda e, ts=ts, dd=dd: e.tensor_copy(
                    x_fm[:, dd, ts * 128:(ts + 1) * 128], psT[:, 0:128]),
                    waits=snap("sP"))
        x_tr = cnt["sV"]

        for dd in range(DD):
            op("scalar", lambda e, dd=dd: e.activation(
                vtmp[:, dd, :], x_fm[:, dd, :], AF.Square),
                waits=(snap("sP", "sV") if dd == 0 else [])
                + (p0c if j == 0 and dd == 0 else []))
        sq_done = cnt["sA"]
        for dd in range(DD):
            op("tensor", lambda e, dd=dd: e.matmul(
                psS2[:], onesr[:], x_fm[:, dd, :], start=(dd == 0), stop=(dd == DD - 1)),
                waits=[("sV", x_tr)] if dd == 0 else [])
        for dd in range(DD):
            op("tensor", lambda e, dd=dd: e.matmul(
                psS1[:], onesr[:], vtmp[:, dd, :], start=(dd == 0), stop=(dd == DD - 1)),
                waits=[("sA", sq_done)] if dd == 0 else [])
        ln_mm = cnt["sP"]

        def ln_stats(e):
            e.tensor_single_scalar(stt2, psS2[:], 1.0 / D, ALU.mult)
            e.tensor_single_scalar(stt1, psS1[:], 1.0 / D, ALU.mult)
            e.tensor_tensor(stt3, stt2, stt2, ALU.mult)
            return e.tensor_tensor(stt1, stt1, stt3, ALU.subtract)
        op("vector", ln_stats, waits=[("sP", ln_mm)])
        op("scalar", lambda e: e.activation(stt1, stt1, AF.Ln, bias=cbias[0:1, 0:1]),
           waits=snap("sV"))
        op("scalar", lambda e: e.activation(stt1, stt1, AF.Exp, scale=-0.5))
        op("vector", lambda e: e.tensor_tensor(stt2, stt2, stt1, ALU.mult),
           waits=snap("sA"))
        ln_st = cnt["sV"]
        op("tensor", lambda e: e.matmul(psB1[:], onesc[:], stt1, start=True,
                                        stop=True), waits=[("sV", ln_st)])
        op("tensor", lambda e: e.matmul(psB2[:], onesc[:], stt2, start=True,
                                        stop=True))
        ln_bc = cnt["sP"]
        op("vector", lambda e: e.tensor_tensor(
            vtmp[:, 0:DD, :], x_fm[:],
            psB1[:].unsqueeze(1).broadcast_to((128, DD, Tc)), ALU.mult),
            waits=[("sP", ln_bc)])
        op("vector", lambda e: e.tensor_tensor(
            vtmp[:, 0:DD, :], vtmp[:, 0:DD, :],
            psB2[:].unsqueeze(1).broadcast_to((128, DD, Tc)), ALU.subtract))
        op("scalar", lambda e: e.activation(
            flat(xn_bf), vt_f[:, 0:DD * Tc], AF.Identity), waits=snap("sV"))
        xn_done = cnt["sV"]
        xn_bf_done = cnt["sA"]
        for q2_ in range(2):
            op("vector", lambda e, q2_=q2_: fq_chain(
                e, vt_f[:, q2_ * 1024:(q2_ + 1) * 1024],
                flat(xnq)[:, q2_ * 1024:(q2_ + 1) * 1024], ci1[:, 0:1024],
                ci2[:, 0:1024], ci1[:, 0:1024],
                yt_f[:, 0:1024], yt_f[:, 1024:2048], ci1[:, 0:1024]))

        op("vector", lambda e: e.tensor_copy(xi[:, :, 0:3], halo[:]),
           waits=snap("sA", "sV"))
        ev_sA = [cnt["sA"], cnt["sA"]]
        for m in range(2 * DC):
            par = m % 2
            pst = psA0 if par == 0 else psA1
            if m == 2:
                wsem, wbuf, wwar = "wC", wb_out[:, 0:DD, :], wbo_last[0]
            elif m == 3:
                wsem, wbuf, wwar = "wD", wb_out[:, DD:2 * DD, :], wbo_last[0]
            elif m == 4:
                wsem, wbuf, wwar = "wI", mkb[:, :, 0:128], 0
            elif m == 5:
                wsem, wbuf, wwar = "wJ", mkb[:, :, 128:256], 0
            elif par == 0:
                wsem, wbuf, wwar = "wA", wb_in[:, 0, :, :], wbin_last[0]
            else:
                wsem, wbuf, wwar = "wB", wb_in[:, 1, :, :], wbin_last[1]
            dma("sync", wbuf, winb_s[m],
                waits=[("sP", wwar)]
                + ([("sA", mk_cv[0])] if m in (4, 5) else [])
                + (p0w if j == 0 else []),
                sem=wsem)
            wv = cnt[wsem]
            for kc in range(DD):
                op("tensor", lambda e, kc=kc, wbuf=wbuf, pst=pst: e.matmul(
                    pst[:], wbuf[:, kc, :], xn_bf[:, kc, :],
                    start=(kc == 0), stop=(kc == DD - 1)),
                    waits=([(wsem, wv), ("sV", xn_done), ("sA", xn_bf_done),
                            ("sA", ev_sA[par])]) if kc == 0 else [])
            if m < DC:
                op("scalar", lambda e, m=m, pst=pst: e.activation(
                    xi[:, m, 3:3 + Tc], pst[:], AF.Identity), waits=snap("sP"))
                ev_sA[par] = cnt["sA"]

                def conv_mc(e, mc=m):
                    e.scalar_tensor_tensor(
                        q_y1, xi[:, mc, 0:Tc], wconv[:, mc, 0:1],
                        bconv[:, mc:mc + 1].broadcast_to((128, Tc)),
                        ALU.mult, ALU.add)
                    for k in range(1, 3):
                        e.scalar_tensor_tensor(
                            q_y1, xi[:, mc, k:k + Tc], wconv[:, mc, k:k + 1],
                            q_y1, ALU.mult, ALU.add)
                    return e.scalar_tensor_tensor(
                        q_y1, xi[:, mc, 3:3 + Tc], wconv[:, mc, 3:4],
                        q_y1, ALU.mult, ALU.add)
                op("vector", conv_mc, waits=snap("sA"))
                op("scalar", lambda e, mc=m: e.activation(
                    xc[:, mc, :], q_y1, AF.Silu), waits=snap("sV"))
            else:
                op("scalar", lambda e, m=m, pst=pst: e.activation(
                    zs[:, m - DC, :], pst[:], AF.Silu), waits=snap("sP"))
                ev_sA[par] = cnt["sA"]
            if m in (2, 3):
                wbo_last[0] = cnt["sP"]
            elif m == 5:
                m45_done = cnt["sP"]
            elif m != 4:
                wbin_last[par] = cnt["sP"]
        wbo_last[1] = cnt["sP"]
        dma("sync", mkb[:], mask_in[:, t0:t0 + Tc].rearrange("(dd p) t -> p dd t", p=128),
            waits=[("sP", m45_done)], sem="wM")
        mk_ld = cnt["wM"]
        mm1_v = cnt["sV"]
        mm1_a = cnt["sA"]
        xc_a = cnt["sA"]
        op("vector", lambda e: e.tensor_copy(halo[:], xi[:, :, Tc:Tc + 3]),
           waits=snap("sV", "sA"))
        op("scalar", lambda e: e.activation(
            xcb_f[:, 0:DC * Tc], xc_f[:, 0:DC * Tc], AF.Identity),
           waits=snap("sV"))
        xcb_done = cnt["sA"]

        for kc in range(DC):
            op("tensor", lambda e, kc=kc: e.matmul(
                psA0[0:RDT, :], wx_sb[:, kc, 0:RDT], xcb[:, kc, :],
                start=(kc == 0), stop=(kc == DC - 1)),
                waits=([("sA", xcb_done)] + snap("sV")) if kc == 0 else [])
        for kc in range(DC):
            op("tensor", lambda e, kc=kc: e.matmul(
                psA1[0:NST, :], wx_sb[:, kc, RDT:RDT + NST], xcb[:, kc, :],
                start=(kc == 0), stop=(kc == DC - 1)))
        for kc in range(DC):
            op("tensor", lambda e, kc=kc: e.matmul(
                psT[0:NST, :], wx_sb[:, kc, RDT + NST:96], xcb[:, kc, :],
                start=(kc == 0), stop=(kc == DC - 1)))
        wx_done = cnt["sP"]
        op("vector", lambda e: e.tensor_copy(dtlow[:], psA0[0:RDT, :]),
           waits=[("sP", wx_done)] + snap("sG"))
        op("vector", lambda e: e.tensor_copy(bsb, psA1[0:NST, :]))
        op("vector", lambda e: e.tensor_copy(csb, psT[0:NST, :]))
        bc_ev = cnt["sV"]
        dma("gpsimd", b_sc[:, t0:t0 + Tc], bsb,
            waits=[("sV", bc_ev)] + (p0s if j == 0 else []))
        dma("gpsimd", c_sc[:, t0:t0 + Tc], csb)
        bc_out = cnt["sG"]
        dtev = [0, 0]
        for mc in range(DC):
            par = mc % 2
            pst = psA0 if par == 0 else psA1
            op("tensor", lambda e, mc=mc, pst=pst: e.matmul(
                pst[:], wdt_sb[:, mc, :], dtlow[:], start=True, stop=True),
                waits=([("sV", bc_ev)] if mc < 2
                       else (snap("sV") + [("sA", dtev[par])])))
            op("scalar", lambda e, mc=mc, pst=pst: e.activation(
                vtmp[:, mc, :], pst[:], AF.Identity, bias=bdt[:, mc:mc + 1]),
                waits=snap("sP"))
            dtev[par] = cnt["sA"]
        CW = DC * Tc
        op("vector", lambda e: e.tensor_single_scalar(
            xi[:, :, 0:Tc], vtmp[:], -1.0, ALU.mult),
            waits=[("sA", dtev[1])])
        op("vector", lambda e: e.tensor_tensor(
            yt[:], vtmp[:], xi[:, :, 0:Tc], ALU.min))
        sp_v = cnt["sV"]
        op("scalar", lambda e: e.activation(yt_f[:, 0:CW], yt_f[:, 0:CW], AF.Exp),
           waits=[("sV", sp_v)])
        op("scalar", lambda e: e.activation(yt_f[:, 0:CW], yt_f[:, 0:CW], AF.Ln,
                                            bias=cbias[:, 1:2]))
        sp_a = cnt["sA"]
        op("vector", lambda e: e.tensor_scalar(
            flat(dtt), vt_f[:, 0:CW], 0.0, 1e-4, ALU.max, ALU.add),
            waits=[("sG", bc_out)])
        op("vector", lambda e: e.tensor_tensor(
            flat(dtt), flat(dtt), yt_f[:, 0:CW], ALU.add), waits=[("sA", sp_a)])
        op("vector", lambda e: e.tensor_tensor(
            xi[:, :, 0:Tc], dtt[:], xc[:], ALU.mult))
        w_done = cnt["sV"]

        fb_prev = [w_done]

        def emit_ffn1_group(g):
            ps4 = (psA0, psA1, psB1, psB2)
            for i in range(4):
                m = 4 * g + i
                par = m % 2
                pst = ps4[i]
                wsem = "wE" if par == 0 else "wF"
                dma("sync", wb_1[:, par, :, :], w1q_s[m],
                    waits=[("sP", wb1_last[par])] + (p0w if j == 0 else []),
                    sem=wsem)
                wv = cnt[wsem]
                for kc in range(DD):
                    op("tensor", lambda e, kc=kc, par=par, pst=pst: e.matmul(
                        pst[:], wb_1[:, par, kc, :], xnq[:, kc, :],
                        start=(kc == 0), stop=(kc == DD - 1)),
                        waits=([(wsem, wv), ("sV", fb_prev[0])])
                        if kc == 0 else [])
                wb1_last[par] = cnt["sP"]
                op("vector", lambda e, i=i, m=m, pst=pst: e.tensor_scalar(
                    vtmp[:, i, :], pst[:], b1q[:, m:m + 1], None, ALU.add),
                    waits=snap("sP"))

            def ffn1b(e, g=g):
                e.tensor_single_scalar(qb_i1, qb_y1.bitcast(I32), 0x7F800000,
                                       ALU.bitwise_and)
                e.tensor_single_scalar(qb_i1, qb_i1, 40 << 23, ALU.max)
                e.tensor_single_scalar(qb_i2, qb_i1, 7 << 23, ALU.subtract)
                e.tensor_scalar(qb_i1, qb_i2, -1, 254 << 23, ALU.mult, ALU.add)
                e.tensor_tensor(qb_m, qb_y1, qb_i1.bitcast(F32), ALU.mult)
                e.tensor_single_scalar(qb_m, qb_m, 768.0, ALU.add)
                e.tensor_single_scalar(qb_i1, qb_m.bitcast(I32), ~0x3FFF,
                                       ALU.bitwise_and)
                e.tensor_tensor(qb_t2, qb_m, qb_i1.bitcast(F32), ALU.subtract)
                e.scalar_tensor_tensor(qb_st, qb_t2, 0.5, qb_i1.bitcast(F32),
                                       ALU.is_ge, ALU.add)
                e.tensor_scalar(qb_t2, qb_st, 768.0, -768.0, ALU.max, ALU.add)
                e.tensor_scalar(qb_i1, qb_i2, 16 << 23, 122 << 23,
                                ALU.add, ALU.min)
                e.tensor_tensor(qb_m, qb_t2, qb_i1.bitcast(F32), ALU.mult)
                e.tensor_single_scalar(qb_m, qb_m, 8.0, ALU.add)
                e.tensor_single_scalar(qb_i1, qb_m.bitcast(I32), ~0xFFFFF,
                                       ALU.bitwise_and)
                e.tensor_tensor(qb_t2, qb_m, qb_i1.bitcast(F32), ALU.subtract)
                e.scalar_tensor_tensor(qb_y1, qb_t2, 0.5, qb_i1.bitcast(F32),
                                       ALU.is_gt, ALU.add)
                e.tensor_single_scalar(qb_i1, qb_i1, 20, ALU.logical_shift_right)
                e.tensor_single_scalar(qb_i1, qb_i1, 1, ALU.bitwise_and)
                e.tensor_copy(qb_m, qb_i1)
                e.tensor_single_scalar(qb_t2, qb_t2, 0.5, ALU.is_equal)
                e.tensor_tensor(qb_t2, qb_t2, qb_m, ALU.mult)
                e.tensor_tensor(qb_y1, qb_y1, qb_t2, ALU.add)
                e.tensor_scalar(qb_i2, qb_i2, 5 << 23, 111 << 23,
                                ALU.add, ALU.max)
                return e.scalar_tensor_tensor(
                    hq[:, 4 * g:4 * g + 4, :].rearrange("p a b -> p (a b)"),
                    qb_y1, -8.0, qb_i2.bitcast(F32), ALU.add, ALU.mult)
            op("vector", ffn1b)
            fb_prev[0] = cnt["sV"]

        NK = 2 * DC
        tmp8 = xrow[:, 0, 0:NH]
        tw = [w_done, w_done]
        dbu_done_k = [0] * NK
        dma("gpsimd", brep[:],
            b_sc[0:NH, t0:t0 + Tc].unsqueeze(0).broadcast_to((128, NH, Tc)),
            waits=[("sG", bc_out)])
        dma("gpsimd", crep[:],
            c_sc[0:NH, t0:t0 + Tc].unsqueeze(0).broadcast_to((128, NH, Tc)))
        rep_ld = cnt["sG"]
        op("gpsimd", lambda e: e.tensor_tensor(
            sB[:], xi[:, 0, 0:Tc].unsqueeze(1).broadcast_to((128, NH, Tc)),
            brep[:], ALU.mult),
            waits=[("sG", rep_ld), ("sV", w_done)])
        dbu_done_k[0] = cnt["sG"]
        rep_b = rep_c = 0
        sc_prev = w_done
        for k in range(NK):
            ng, dc = divmod(k, DC)
            n0 = ng * NH
            sBb = sB if k % 2 == 0 else sB2
            sBn = sB2 if k % 2 == 0 else sB
            if k == DC - 1:
                dma("gpsimd", brep[:],
                    b_sc[NH:NST, t0:t0 + Tc].unsqueeze(0)
                    .broadcast_to((128, NH, Tc)), sem="wA")
                rep_b = cnt["wA"]
            if k == DC:
                dma("gpsimd", crep[:],
                    c_sc[NH:NST, t0:t0 + Tc].unsqueeze(0)
                    .broadcast_to((128, NH, Tc)), sem="wB",
                    waits=[("sV", tw[(k + 1) % 2])])
                rep_c = cnt["wB"]
            for n in range(NH):
                op("scalar", lambda e, dc=dc, n=n, n0=n0: e.activation(
                    sA[:, n, :], dtt[:, dc, :], AF.Exp,
                    scale=asb[:, dc * NST + n0 + n:dc * NST + n0 + n + 1]),
                    waits=[("sV", sc_prev)] if n == 0 else [])
            exp_done = cnt["sA"]
            if k + 1 < NK and k + 1 != DC:
                dcn = (k + 1) % DC
                op("gpsimd", lambda e, dcn=dcn, sBn=sBn: e.tensor_tensor(
                    sBn[:],
                    xi[:, dcn, 0:Tc].unsqueeze(1).broadcast_to((128, NH, Tc)),
                    brep[:], ALU.mult),
                    waits=[("sV", tw[(k + 1) % 2])])
                dbu_done_k[k + 1] = cnt["sG"]
            op("vector", lambda e, dc=dc, n0=n0: e.tensor_tensor(
                tmp8, sA[:, :, 0], hstate[:, dc, n0:n0 + NH], ALU.mult),
                waits=[("sA", exp_done), ("sG", dbu_done_k[k])] + snap("sV"))
            op("vector", lambda e, sBb=sBb: e.tensor_tensor(
                sBb[:, :, 0], sBb[:, :, 0], tmp8, ALU.add), waits=snap("sV"))
            op("vector", lambda e: e.memset(sA[:, :, 0:1], 0.0),
               waits=snap("sV"))
            op("vector", lambda e, sBb=sBb: e.tensor_tensor_scan(
                flat(sBb), flat(sA), flat(sBb), 0.0, ALU.mult, ALU.add),
                waits=snap("sV"))
            sc_prev = cnt["sV"]
            op("vector", lambda e, dc=dc, n0=n0, sBb=sBb: e.tensor_copy(
                hstate[:, dc, n0:n0 + NH], sBb[:, :, Tc - 1]),
                waits=snap("sV"))
            op("vector", lambda e, sBb=sBb: e.tensor_tensor(
                flat(sBb), flat(sBb), flat(crep), ALU.mult),
                waits=[("wB", rep_c)] if k == DC else [])
            op("vector", lambda e, sBb=sBb: e.tensor_tensor(
                sBb[:, 0:4, :].rearrange("p a b -> p (a b)"),
                sBb[:, 0:4, :].rearrange("p a b -> p (a b)"),
                sBb[:, 4:8, :].rearrange("p a b -> p (a b)"), ALU.add))
            op("vector", lambda e, sBb=sBb: e.tensor_tensor(
                sBb[:, 0:2, :].rearrange("p a b -> p (a b)"),
                sBb[:, 0:2, :].rearrange("p a b -> p (a b)"),
                sBb[:, 2:4, :].rearrange("p a b -> p (a b)"), ALU.add))
            if ng == 0:
                op("vector", lambda e, dc=dc, sBb=sBb: e.tensor_tensor(
                    yt[:, dc, :], sBb[:, 0, :], sBb[:, 1, :], ALU.add))
            else:
                op("vector", lambda e, dc=dc, sBb=sBb: e.tensor_tensor(
                    sBb[:, 0, :], sBb[:, 0, :], sBb[:, 1, :], ALU.add))
                op("vector", lambda e, dc=dc, sBb=sBb: e.tensor_tensor(
                    yt[:, dc, :], yt[:, dc, :], sBb[:, 0, :], ALU.add))
            tw[k % 2] = cnt["sV"]
            if k + 1 == DC:
                op("gpsimd", lambda e, sBn=sBn: e.tensor_tensor(
                    sBn[:],
                    xi[:, 0, 0:Tc].unsqueeze(1).broadcast_to((128, NH, Tc)),
                    brep[:], ALU.mult),
                    waits=[("wA", rep_b), ("sV", tw[(k + 1) % 2])])
                dbu_done_k[k + 1] = cnt["sG"]
            if k % 4 == 3:
                emit_ffn1_group(k // 4)
        ssm_done = cnt["sV"]
        hq_done = cnt["sV"]

        for dc in range(DC):
            op("vector", lambda e, dc=dc: e.scalar_tensor_tensor(
                yt[:, dc, :], xc[:, dc, :], dvec[:, dc:dc + 1], yt[:, dc, :],
                ALU.mult, ALU.add))
        op("vector", lambda e: e.tensor_tensor(
            yt_f[:, 0:CW], yt_f[:, 0:CW], flat(zs), ALU.mult))
        y_done = cnt["sV"]
        op("scalar", lambda e: e.activation(vt_f[:, 0:CW], yt_f[:, 0:CW], AF.Square),
           waits=[("sV", y_done)])
        y2_done = cnt["sA"]
        for dc in range(DC):
            op("tensor", lambda e, dc=dc: e.matmul(
                psS1[:], onesr[:], vtmp[:, dc, :], start=(dc == 0), stop=(dc == DC - 1)),
                waits=[("sA", y2_done)] if dc == 0 else [])
        rms_mm = cnt["sP"]
        op("vector", lambda e: e.tensor_single_scalar(stt1, psS1[:], 1.0 / E,
                                                      ALU.mult), waits=[("sP", rms_mm)])
        rv = cnt["sV"]
        op("scalar", lambda e: e.activation(stt1, stt1, AF.Ln, bias=cbias[0:1, 2:3]),
           waits=[("sV", rv)])
        op("scalar", lambda e: e.activation(stt1, stt1, AF.Exp, scale=-0.5))
        rsa = cnt["sA"]
        op("tensor", lambda e: e.matmul(psB1[:], onesc[:], stt1, start=True,
                                        stop=True), waits=[("sA", rsa)])
        rms_bc = cnt["sP"]
        for dc in range(DC):
            op("vector", lambda e, dc=dc: e.scalar_tensor_tensor(
                xcb[:, dc, :], yt[:, dc, :], rmsw[:, dc:dc + 1], psB1[:],
                ALU.mult, ALU.mult),
                waits=[("sP", rms_bc)] if dc == 0 else [])
        yrms_done = cnt["sV"]
        wbo_bufs = (wb_out[:], wb_in[:].rearrange("p a b c -> p (a b) c"))
        for md in range(DD):
            par = md % 2
            bufv = wbo_bufs[par]
            wsem = "wC" if par == 0 else "wD"
            dma("sync", bufv, woutb_s[md],
                waits=[("sP", wbo_last[par])] + (p0w if j == 0 else []),
                sem=wsem)
            wv = cnt[wsem]
            for kc in range(DC):
                op("tensor", lambda e, kc=kc, md=md, bufv=bufv: e.matmul(
                    psA0[:] if md % 2 == 0 else psA1[:],
                    bufv[:, kc, :], xcb[:, kc, :],
                    start=(kc == 0), stop=(kc == DC - 1)),
                    waits=([(wsem, wv), ("sV", yrms_done)] + snap("sV"))
                    if kc == 0 else [])
            wbo_last[par] = cnt["sP"]
            op("vector", lambda e, md=md: e.tensor_tensor(
                x_fm[:, md, :], x_fm[:, md, :],
                psA0[:] if md % 2 == 0 else psA1[:], ALU.add),
                waits=snap("sP"))
        wbin_last[0] = wbin_last[1] = cnt["sP"]
        attn_done = cnt["sV"]

        op("scalar", lambda e: e.activation(
            yt[:, 0:DD, :].rearrange("p a b -> p (a b)"), flat(mkb), AF.Identity),
           waits=[("wM", mk_ld)] + snap("sV"))
        mk_f32 = cnt["sA"]
        mk_cv[0] = mk_f32
        for md in range(DD):
            dma("sync", wb_2[:, 0:MC1 // 2, :], w2q_s[md, :, 0:MC1 // 2, :],
                waits=[("sP", wb2_last[0])] + (p0w if j == 0 else []),
                sem="wG")
            wv0 = cnt["wG"]
            dma("sync", wb_2[:, MC1 // 2:MC1, :], w2q_s[md, :, MC1 // 2:MC1, :],
                waits=[("sP", wb2_last[1])], sem="wH")
            wv1 = cnt["wH"]
            for kc in range(MC1):
                op("tensor", lambda e, kc=kc, md=md: e.matmul(
                    psA0[:] if md % 2 == 0 else psA1[:], wb_2[:, kc, :],
                    hq[:, kc, :],
                    start=(kc == 0), stop=(kc == MC1 - 1)),
                    waits=([("wG", wv0), ("sV", hq_done)] + snap("sV"))
                    if kc == 0 else ([("wH", wv1)] if kc == MC1 // 2 else []))
                if kc == MC1 // 2 - 1:
                    wb2_last[0] = cnt["sP"]
            wb2_last[1] = cnt["sP"]
            op("vector", lambda e, md=md: e.tensor_scalar(
                vtmp[:, md % 4, :],
                (psA0 if md % 2 == 0 else psA1)[:],
                b2q[:, md:md + 1], None, ALU.add),
                waits=snap("sP"))

            if md % 4 == 3:
                g = md // 4

                def ffn2b(e, g=g):
                    e.tensor_single_scalar(qb_i1, qb_y1.bitcast(I32),
                                           0x7F800000, ALU.bitwise_and)
                    e.tensor_single_scalar(qb_i1, qb_i1, 40 << 23, ALU.max)
                    e.tensor_single_scalar(qb_i2, qb_i1, 7 << 23, ALU.subtract)
                    e.tensor_scalar(qb_i1, qb_i2, -1, 254 << 23,
                                    ALU.mult, ALU.add)
                    e.tensor_tensor(qb_m, qb_y1, qb_i1.bitcast(F32), ALU.mult)
                    e.tensor_single_scalar(qb_m, qb_m, 768.0, ALU.add)
                    e.tensor_single_scalar(qb_i1, qb_m.bitcast(I32), ~0x3FFF,
                                           ALU.bitwise_and)
                    e.tensor_tensor(qb_t2, qb_m, qb_i1.bitcast(F32),
                                    ALU.subtract)
                    e.scalar_tensor_tensor(qb_st, qb_t2, 0.5,
                                           qb_i1.bitcast(F32),
                                           ALU.is_ge, ALU.add)
                    e.scalar_tensor_tensor(qb_t2, qb_st, -768.0,
                                           qb_i2.bitcast(F32),
                                           ALU.add, ALU.mult)
                    e.tensor_tensor(
                        qb_t2, qb_t2,
                        yt[:, 4 * g:4 * g + 4, :].rearrange("p a b -> p (a b)"),
                        ALU.mult)
                    return e.tensor_tensor(
                        x_fm[:, 4 * g:4 * g + 4, :].rearrange("p a b -> p (a b)"),
                        x_fm[:, 4 * g:4 * g + 4, :].rearrange("p a b -> p (a b)"),
                        qb_t2, ALU.add)
                op("vector", ffn2b,
                   waits=[("sA", mk_f32)] if md == 3 else [])
        ffn_done = cnt["sV"]

        for ts in range(2):
            for dd in range(DD):
                op("tensor", lambda e, ts=ts, dd=dd: e.transpose(
                    psT[:, 0:128], x_fm[:, dd, ts * 128:(ts + 1) * 128], ident[:]),
                    waits=[("sV", ffn_done)] + snap("sV", "sG", "sA"))
                op("vector", lambda e, dd=dd: e.tensor_copy(
                    xrow[:, dd, :], psT[:, 0:128]), waits=snap("sP"))
            op("gpsimd", lambda e, ts=ts, j=j: e.dma_start(
                out=y_dram[j, ts], in_=xrow[:]), waits=snap("sV"), d=16)

    if dbg == 2:
        o = nc.dram_tensor("d_dummy", [128, NST], F32, kind="ExternalOutput").ap()
        dma("gpsimd", o, hstate[:, 0, :], waits=snap("sV", "sA", "sP", "sS"))
    if dbg is True:
        dbg_tiles = {
            "d_xfm": (x_fm, F32), "d_xnbf": (xn_bf, BF16), "d_xnq": (xnq, FP8),
            "d_xi": (xi, F32), "d_xc": (xc, F32), "d_zs": (zs, F32),
            "d_dtt": (dtt, F32), "d_yt": (yt, F32), "d_xcb": (xcb, BF16),
            "d_hstate": (hstate, F32), "d_dtlow": (dtlow, BF16),
            "d_brep": (brep, F32),
            "d_crep": (crep, F32), "d_sA": (sA, F32), "d_sB": (sB, F32),
            "d_vtmp": (vtmp, F32), "d_b1q": (b1q, F32), "d_asb": (asb, F32),
            "d_hq": (hq, FP8),
        }
        for name, (tile, dt_) in dbg_tiles.items():
            o = nc.dram_tensor(name, list(tile[:].shape), dt_, kind="ExternalOutput").ap()
            dma("gpsimd", o, tile[:], waits=snap("sV", "sA", "sP", "sS"))
    final_g = cnt["sG"]

    with nc.Block() as block:
        def run_engine(ename):
            def body(eng):
                for (tag, waits, fn, inc) in sched:
                    if tag != ename:
                        continue
                    for (s, v) in waits:
                        eng.wait_ge(sems[s], v)
                    inst = fn(eng)
                    if inc is not None:
                        inst.then_inc(sems[inc[0]], inc[1])
                if ename == "gpsimd":
                    eng.wait_ge(sems["sG"], final_g)
            return body
        block.sync(run_engine("sync"))
        block.gpsimd(run_engine("gpsimd"))
        block.tensor(run_engine("tensor"))
        block.vector(run_engine("vector"))
        block.scalar(run_engine("scalar"))
    return nc


def _host_inputs(inputs):
    f32 = np.float32
    x = np.ascontiguousarray(np.asarray(inputs["L0_attn_x"], f32))
    W_in = np.asarray(inputs["L0_attn_W_in"], f32)
    W_conv = np.asarray(inputs["L0_attn_W_conv"], f32)
    b_conv = np.asarray(inputs["L0_attn_b_conv"], f32)
    W_x = np.asarray(inputs["L0_attn_W_x"], f32)
    W_dt = np.asarray(inputs["L0_attn_W_dt"], f32)
    b_dt = np.asarray(inputs["L0_attn_b_dt"], f32)
    A_log = np.asarray(inputs["L0_attn_A_log"], f32)
    Dv = np.asarray(inputs["L0_attn_D"], f32)
    rms_w = np.asarray(inputs["L0_attn_rms_w"], f32)
    W_out = np.asarray(inputs["L0_attn_W_out"], f32)
    W1 = np.asarray(inputs["L0_ffn_W1"], f32)
    b1 = np.asarray(inputs["L0_ffn_b1"], f32)
    W2 = np.asarray(inputs["L0_ffn_W2"], f32)
    b2 = np.asarray(inputs["L0_ffn_b2"], f32)

    import os
    import jax
    with jax.default_device(jax.devices("cpu")[0]):
        want_x64 = os.environ.get("BASS_SIM_FP64") == "1"
        had_x64 = jax.config.jax_enable_x64
        if want_x64 and not had_x64:
            jax.config.update("jax_enable_x64", True)
        try:
            keep = np.asarray(jax.random.bernoulli(
                jax.random.key(42, impl="threefry2x32"), 0.8, (8, T, D)))
        finally:
            if want_x64 and not had_x64:
                jax.config.update("jax_enable_x64", False)
    maskv = np.where(keep, np.float32(1.25), np.float32(0.0)).astype(ml_dtypes.bfloat16)

    def pd(v, chunks):
        return np.ascontiguousarray(np.reshape(v, (chunks, 128)).T)

    def tile4(wT, KB, MB):
        return np.ascontiguousarray(
            wT.reshape(KB, 128, MB, 128).transpose(2, 1, 0, 3))

    shared = {
        "w_inT": tile4(W_in.T, DD, 2 * E // 128),
        "w_xT": np.ascontiguousarray(W_x.T),
        "w_dtT": np.ascontiguousarray(W_dt.T),
        "w_outT": tile4(W_out.T, DC, DD),
        "w1T": tile4(W1.T, DD, MC1),
        "w2T": tile4(W2.T, MC1, DD),
        "bconv_in": pd(b_conv, DC),
        "wconv_in": np.ascontiguousarray(W_conv.reshape(DC, 128, 4).transpose(1, 0, 2)),
        "bdt_in": pd(b_dt, DC),
        "alog_in": np.ascontiguousarray(
            A_log.reshape(DC, 128, NST).transpose(1, 0, 2).reshape(128, DC * NST)),
        "dvec_in": pd(Dv, DC),
        "rmsw_in": pd(rms_w, DC),
        "b1_in": pd(b1, MC1),
        "b2_in": pd(b2, DD),
        "ident_in": np.eye(128, dtype=f32),
        "onesc_in": np.ones((1, 128), f32),
        "onesr_in": np.ones((128, 1), f32),
    }
    maps = []
    for c in range(8):
        m = dict(shared)
        m["x_in"] = x[c]
        m["mask_in"] = np.ascontiguousarray(maskv[c].T)
        maps.append(m)
    return maps


def kernel(**inputs):
    maps = _host_inputs(inputs)
    nc = build_core()
    res = run_bass_kernel_spmd(nc, maps, core_ids=list(range(8)))
    out = np.stack([np.asarray(r["y_out"]) for r in res.results], axis=0)
    return np.ascontiguousarray(out.astype(np.float32))


# version 21 (final)
# speedup vs baseline: 1.0054x; 1.0054x over previous
import numpy as np
import ml_dtypes
import concourse.bass as bass
import concourse.mybir as mybir
from concourse.bass_utils import run_bass_kernel_spmd

F32 = mybir.dt.float32
I32 = mybir.dt.int32
BF16 = mybir.dt.bfloat16
FP8 = mybir.dt.float8e5
AF = mybir.ActivationFunctionType
ALU = mybir.AluOpType

T, D, E, NST, RDT = 4096, 1024, 2048, 16, 64
D4 = 4 * D
Tc = 256
DC = E // 128
DD = D // 128
MC1 = D4 // 128
NH = NST // 2


def build_core(nch=T // Tc, dbg=False, paranoid=False):
    nc = bass.Bass("TRN2", target_bir_lowering=False, detect_race_conditions=False)

    x_in = nc.dram_tensor("x_in", [T, D], F32, kind="ExternalInput").ap()
    w_inT = nc.dram_tensor("w_inT", [2 * E // 128, 128, DD, 128], F32,
                           kind="ExternalInput").ap()
    w_xT = nc.dram_tensor("w_xT", [E, 96], F32, kind="ExternalInput").ap()
    w_dtT = nc.dram_tensor("w_dtT", [RDT, E], F32, kind="ExternalInput").ap()
    w_outT = nc.dram_tensor("w_outT", [DD, 128, DC, 128], F32,
                            kind="ExternalInput").ap()
    w1T = nc.dram_tensor("w1T", [MC1, 128, DD, 128], F32,
                         kind="ExternalInput").ap()
    w2T = nc.dram_tensor("w2T", [DD, 128, MC1, 128], F32,
                         kind="ExternalInput").ap()
    bconv_in = nc.dram_tensor("bconv_in", [128, DC], F32, kind="ExternalInput").ap()
    wconv_in = nc.dram_tensor("wconv_in", [128, DC, 4], F32, kind="ExternalInput").ap()
    bdt_in = nc.dram_tensor("bdt_in", [128, DC], F32, kind="ExternalInput").ap()
    alog_in = nc.dram_tensor("alog_in", [128, DC * NST], F32, kind="ExternalInput").ap()
    dvec_in = nc.dram_tensor("dvec_in", [128, DC], F32, kind="ExternalInput").ap()
    rmsw_in = nc.dram_tensor("rmsw_in", [128, DC], F32, kind="ExternalInput").ap()
    b1_in = nc.dram_tensor("b1_in", [128, MC1], F32, kind="ExternalInput").ap()
    b2_in = nc.dram_tensor("b2_in", [128, DD], F32, kind="ExternalInput").ap()
    ident_in = nc.dram_tensor("ident_in", [128, 128], F32, kind="ExternalInput").ap()
    onesc_in = nc.dram_tensor("onesc_in", [1, 128], F32, kind="ExternalInput").ap()
    onesr_in = nc.dram_tensor("onesr_in", [128, 1], F32, kind="ExternalInput").ap()
    mask_in = nc.dram_tensor("mask_in", [D, T], BF16, kind="ExternalInput").ap()
    y_out = nc.dram_tensor("y_out", [T, D], F32, kind="ExternalOutput").ap()

    winb_s = nc.dram_tensor("winb_s", [2 * E // 128, 128, DD, 128], BF16,
                            kind="ExternalOutput").ap()
    woutb_s = nc.dram_tensor("woutb_s", [DD, 128, DC, 128], BF16,
                             kind="ExternalOutput").ap()
    w1q_s = nc.dram_tensor("w1q_s", [MC1, 128, DD, 128], FP8,
                           kind="ExternalOutput").ap()
    w2q_s = nc.dram_tensor("w2q_s", [DD, 128, MC1, 128], FP8,
                           kind="ExternalOutput").ap()
    b_sc = nc.dram_tensor("b_sc", [NST, T], F32, kind="ExternalOutput").ap()
    c_sc = nc.dram_tensor("c_sc", [NST, T], F32, kind="ExternalOutput").ap()

    sbuf = nc.alloc_sbuf_tensor
    x_fm = sbuf("x_fm", [128, DD, Tc], F32)
    xrow = sbuf("xrow", [128, DD, 128], F32)
    xn_bf = sbuf("xn_bf", [128, DD, Tc], BF16)
    xnq = sbuf("xnq", [128, DD, Tc], FP8)
    xi = sbuf("xi", [128, DC, Tc + 3], F32)
    xc = sbuf("xc", [128, DC, Tc], F32)
    xcb = sbuf("xcb", [128, DC, Tc], BF16)
    zs = sbuf("zs", [128, DC, Tc], F32)
    dtt = sbuf("dtt", [128, DC, Tc], F32)
    sA = sbuf("sA", [128, NH, Tc], F32)
    sB = sbuf("sB", [128, NH, Tc], F32)
    sB2 = sbuf("sB2", [128, NH, Tc], F32)
    yt = sbuf("yt", [128, DC, Tc], F32)
    brep = sbuf("brep", [128, NH, Tc], F32)
    crep = sbuf("crep", [128, NH, Tc], F32)
    vtmp = sbuf("vtmp", [128, DC, Tc], F32)
    mkb = sbuf("mkb", [128, DD, Tc], BF16)
    hstate = sbuf("hstate", [128, DC, NST], F32)
    halo = sbuf("halo", [128, DC, 3], F32)
    dtlow = sbuf("dtlow", [RDT, Tc], BF16)
    bsb = dtt[0:NST, 0, 0:Tc]
    csb = dtt[0:NST, 1, 0:Tc]
    stt = xrow[:].rearrange("p a b -> p (a b)")[0:1, 0:3 * Tc]
    stt1 = stt[:, 0:Tc]
    stt2 = stt[:, Tc:2 * Tc]
    stt3 = stt[:, 2 * Tc:3 * Tc]
    ci1 = sbuf("ci1", [128, 1024], I32)
    ci2 = sbuf("ci2", [128, 1024], I32)
    hq = sbuf("hq", [128, MC1, Tc], FP8)
    q_eb = ci1[:, 0:Tc]
    q_bs = ci1[:, Tc:2 * Tc]
    q_br = ci1[:, 2 * Tc:3 * Tc]
    q_fl = ci1[:, 3 * Tc:4 * Tc]

    wb_in = sbuf("wb_in", [128, 2, DD, 128], BF16)
    wb_out = sbuf("wb_out", [128, DC, 128], BF16)
    wb_1 = sbuf("wb_1", [128, 2, DD, 128], FP8)
    wb_2 = sbuf("wb_2", [128, MC1, 128], FP8)
    wx_sb = sbuf("wx_sb", [128, DC, 96], BF16)
    wdt_sb = sbuf("wdt_sb", [RDT, DC, 128], BF16)
    bconv = sbuf("bconv", [128, DC], F32)
    wconv = sbuf("wconv", [128, DC, 4], F32)
    bdt = sbuf("bdt", [128, DC], F32)
    asb = sbuf("asb", [128, DC * NST], F32)
    dvec = sbuf("dvec", [128, DC], F32)
    rmsw = sbuf("rmsw", [128, DC], F32)
    b1q = sbuf("b1q", [128, MC1], F32)
    b2q = sbuf("b2q", [128, DD], F32)
    ident = sbuf("ident", [128, 128], F32)
    onesc = sbuf("onesc", [1, 128], F32)
    onesr = sbuf("onesr", [128, 1], F32)
    cbias = sbuf("cbias", [128, 3], F32)

    q_y1 = vtmp[:, 0, :]
    q_m = vtmp[:, 1, :]
    q_st = vtmp[:, 2, :]
    q_t2 = vtmp[:, 3, :]
    qb_y1 = vtmp[:, 0:4, :].rearrange("p a b -> p (a b)")
    qb_m = vtmp[:, 4:8, :].rearrange("p a b -> p (a b)")
    qb_st = vtmp[:, 8:12, :].rearrange("p a b -> p (a b)")
    qb_t2 = vtmp[:, 12:16, :].rearrange("p a b -> p (a b)")
    qb_i1 = ci1[:, 0:1024]
    qb_i2 = ci2[:, 0:1024]
    def flat(t):
        return t[:].rearrange("p a b -> p (a b)")
    vt_f = flat(vtmp)
    yt_f = flat(yt)
    xc_f = flat(xc)

    xcb_f = flat(xcb)


    psum = nc.alloc_psum_tensor
    psA0 = psum("psA0", [128, Tc], F32)
    psA1 = psum("psA1", [128, Tc], F32)
    psT = psum("psT", [128, Tc], F32)
    psS1 = psum("psS1", [1, Tc], F32)
    psS2 = psum("psS2", [1, Tc], F32)
    psB1 = psum("psB1", [128, Tc], F32)
    psB2 = psum("psB2", [128, Tc], F32)

    sem_names = ["sS", "sG", "sP", "sV", "sA",
                 "wA", "wB", "wC", "wD", "wE", "wF", "wG", "wH", "wM"]
    sems = {n: nc.alloc_semaphore(n) for n in sem_names}
    cnt = {n: 0 for n in sem_names}
    sched = []
    esem = {"sync": "sS", "gpsimd": "sG", "tensor": "sP", "vector": "sV", "scalar": "sA"}

    def snap(*ss):
        return [(s, cnt[s]) for s in ss]

    def op(eng, fn, waits=(), inc=True, d=1, sem=None):
        if paranoid:
            waits = list(waits) + snap("sS", "sG", "sP", "sV", "sA")
        best = {}
        for (s, v) in waits:
            if v > best.get(s, 0):
                best[s] = v
        w = [(s, v) for s, v in best.items() if v > 0]
        s = sem if sem is not None else (esem[eng] if inc else None)
        if s:
            cnt[s] += d
        sched.append((eng, w, fn, (s, d) if s else None))

    def dma(eng, out, in_, waits=(), sem=None, **kw):
        op(eng, lambda e, o=out, i=in_, k=kw: e.dma_start(out=o, in_=i, **k),
           waits, d=16, sem=sem)

    def fq_chain(e, src, dst, eb, bs, br, t1, t2, fl):
        e.tensor_single_scalar(eb, src.bitcast(I32), 0x7F800000, ALU.bitwise_and)
        e.tensor_single_scalar(eb, eb, 113 << 23, ALU.max)
        e.tensor_single_scalar(bs, eb, 2 << 23, ALU.subtract)
        e.tensor_scalar(br, bs, -1, 254 << 23, ALU.mult, ALU.add)
        e.tensor_tensor(t1, src, br.bitcast(F32), ALU.mult)
        e.tensor_single_scalar(t1, t1, 24.0, ALU.add)
        e.tensor_single_scalar(fl, t1.bitcast(I32), ~0x7FFFF, ALU.bitwise_and)
        e.tensor_tensor(t2, t1, fl.bitcast(F32), ALU.subtract)
        e.tensor_single_scalar(dst, t2, 0.5, ALU.is_equal)
        e.scalar_tensor_tensor(t2, t2, 0.5, fl.bitcast(F32), ALU.is_gt, ALU.add)
        e.tensor_single_scalar(br, fl, 19, ALU.logical_shift_right)
        e.tensor_single_scalar(br, br, 1, ALU.bitwise_and)
        e.tensor_copy(t1, br)
        e.tensor_tensor(t1, t1, dst, ALU.mult)
        e.tensor_tensor(t2, t2, t1, ALU.add)
        return e.scalar_tensor_tensor(dst, t2, -24.0, bs.bitcast(F32), ALU.add, ALU.mult)

    for o, i in [(bconv[:], bconv_in), (wconv[:], wconv_in), (bdt[:], bdt_in),
                 (dvec[:], dvec_in), (rmsw[:], rmsw_in), (b1q[:], b1_in),
                 (b2q[:], b2_in), (ident[:], ident_in), (onesc[:], onesc_in),
                 (onesr[:], onesr_in), (asb[:], alog_in)]:
        dma("sync", o, i)
    dma("sync", vtmp[:, :, 0:96], w_xT.rearrange("(kc p) c -> p kc c", p=128))
    dma("sync", yt_f[0:RDT, 0:DC * 128], w_dtT)

    op("vector", lambda e: e.memset(cbias[:, 0:1], 1e-5))
    op("vector", lambda e: e.memset(cbias[:, 1:2], 1.0))
    op("vector", lambda e: e.memset(cbias[:, 2:3], 1e-6))
    op("scalar", lambda e: e.activation(asb[:], asb[:], AF.Exp), waits=snap("sS"))
    op("vector", lambda e: e.tensor_single_scalar(asb[:], asb[:], -1.0, ALU.mult),
       waits=snap("sA"))
    op("vector", lambda e: e.tensor_copy(wx_sb[:], vtmp[:, :, 0:96]),
       waits=snap("sS"))
    op("vector", lambda e: e.tensor_copy(flat(wdt_sb), yt_f[0:RDT, 0:DC * 128]))

    p0c = snap("sS", "sG", "sP", "sV", "sA")
    cf = vt_f
    ct1 = yt_f
    ct2 = xc_f
    cb = xcb_f
    cq = flat(hq)
    fl2 = lambda a: a.rearrange("p a b -> p (a b)")
    cvd = [cnt["sV"], cnt["sV"]]
    std = [0, 0]
    def p0_stream(n, width, src_fn, dst_fn, kind):
        for i in range(n):
            s = i % 2
            o = s * 2048
            lsem = "wA" if s == 0 else "wB"
            ssem = "wC" if s == 0 else "wD"
            dma("sync", cf[:, o:o + width], src_fn(i), waits=[("sV", cvd[s])],
                sem=lsem)
            ld = cnt[lsem]
            ob = cb if kind == "bf" else cq
            if kind == "bf":
                op("vector", lambda e, o=o, width=width: e.tensor_copy(
                    cb[:, o:o + width], cf[:, o:o + width]),
                   waits=[(lsem, ld), (ssem, std[s])])
            else:
                for q in range(width // D):
                    op("vector", lambda e, o=o, q=q: fq_chain(
                        e, cf[:, o + q * D:o + (q + 1) * D],
                        cq[:, o + q * D:o + (q + 1) * D], ci1[:, 0:D],
                        ci2[:, 0:D], ci1[:, 0:D], ct1[:, 0:D], ct2[:, 0:D],
                        ci1[:, 0:D]),
                       waits=[(lsem, ld), (ssem, std[s])] if q == 0 else [])
            cvd[s] = cnt["sV"]
            dma("gpsimd", dst_fn(i), ob[:, o:o + width], waits=[("sV", cvd[s])],
                sem=ssem)
            std[s] = cnt[ssem]
    p0_stream(2 * E // 128, D, lambda i: fl2(w_inT[i]),
              lambda i: fl2(winb_s[i]), "bf")
    p0_stream(DD, 2 * D, lambda i: fl2(w_outT[i]),
              lambda i: fl2(woutb_s[i]), "bf")
    p0_stream(MC1, D, lambda i: fl2(w1T[i]),
              lambda i: fl2(w1q_s[i]), "fq")
    p0_stream(4 * DD, D,
              lambda i: fl2(w2T[i // 4])[:, (i % 4) * D:(i % 4 + 1) * D],
              lambda i: fl2(w2q_s[i // 4])[:, (i % 4) * D:(i % 4 + 1) * D],
              "fq")
    def fq_steps(src_, dst_, eb, bs, br, t1, t2, fl):
        return [
            lambda e: e.tensor_single_scalar(eb, src_.bitcast(I32), 0x7F800000,
                                             ALU.bitwise_and),
            lambda e: e.tensor_single_scalar(eb, eb, 113 << 23, ALU.max),
            lambda e: e.tensor_single_scalar(bs, eb, 2 << 23, ALU.subtract),
            lambda e: e.tensor_scalar(br, bs, -1, 254 << 23, ALU.mult, ALU.add),
            lambda e: e.tensor_tensor(t1, src_, br.bitcast(F32), ALU.mult),
            lambda e: e.tensor_single_scalar(t1, t1, 24.0, ALU.add),
            lambda e: e.tensor_single_scalar(fl, t1.bitcast(I32), ~0x7FFFF,
                                             ALU.bitwise_and),
            lambda e: e.tensor_tensor(t2, t1, fl.bitcast(F32), ALU.subtract),
            lambda e: e.tensor_single_scalar(dst_, t2, 0.5, ALU.is_equal),
            lambda e: e.scalar_tensor_tensor(t2, t2, 0.5, fl.bitcast(F32),
                                             ALU.is_gt, ALU.add),
            lambda e: e.tensor_single_scalar(br, fl, 19, ALU.logical_shift_right),
            lambda e: e.tensor_single_scalar(br, br, 1, ALU.bitwise_and),
            lambda e: e.tensor_copy(t1, br),
            lambda e: e.tensor_tensor(t1, t1, dst_, ALU.mult),
            lambda e: e.tensor_tensor(t2, t2, t1, ALU.add),
            lambda e: e.scalar_tensor_tensor(dst_, t2, -24.0, bs.bitcast(F32),
                                             ALU.add, ALU.mult),
        ]
    first = True
    for st_fn in fq_steps(b1q[:], b1q[:], q_eb[:, 0:MC1], q_bs[:, 0:MC1],
                          q_br[:, 0:MC1], q_m[:, 0:MC1], q_t2[:, 0:MC1],
                          q_fl[:, 0:MC1]):
        op("vector", st_fn, waits=snap("sS", "sG", "sV") if first else snap("sV"))
        first = False
    for st_fn in fq_steps(b2q[:], b2q[:], q_eb[:, 0:DD], q_bs[:, 0:DD],
                          q_br[:, 0:DD], q_m[:, 0:DD], q_t2[:, 0:DD],
                          q_fl[:, 0:DD]):
        op("vector", st_fn, waits=snap("sV"))
    op("vector", lambda e: e.memset(hstate[:], 0.0))
    op("vector", lambda e: e.memset(halo[:], 0.0))
    p0s = snap("sS", "sG", "sP", "sV", "sA")
    p0w = snap("wC", "wD")
    wbin_last = [0, 0]
    wbo_last = [0, 0]
    wb1_last = [0, 0]
    wb2_last = [0, 0]
    wbo_bufs = None

    prev_rms_bc = 0
    prev_st_g = 0
    for j in range(nch):
        t0 = j * Tc
        x_dram = x_in.rearrange("(tt ts p) (dd f) -> tt ts p dd f", p=128, ts=2, f=128)
        y_dram = y_out.rearrange("(tt ts p) (dd f) -> tt ts p dd f", p=128, ts=2, f=128)

        for ts in range(2):
            dma("sync", xrow[:], x_dram[j, ts],
                waits=(snap("sP") if ts == 1 else
                       ([("sP", prev_rms_bc)] if j > 0 else []))
                + (p0c if j == 0 else []))
            x_ld = cnt["sS"]
            for dd in range(DD):
                op("tensor", lambda e, dd=dd: e.transpose(
                    psT[:, 0:128], xrow[:, dd, :], ident[:]),
                    waits=[("sS", x_ld)] + snap("sV", "sA") + (p0c if j == 0 else []))
                op("vector", lambda e, ts=ts, dd=dd: e.tensor_copy(
                    x_fm[:, dd, ts * 128:(ts + 1) * 128], psT[:, 0:128]),
                    waits=snap("sP"))
        dma("sync", mkb[:], mask_in[:, t0:t0 + Tc].rearrange("(dd p) t -> p dd t", p=128),
            waits=snap("sV", "sA"), sem="wM")
        mk_ld = cnt["wM"]
        x_tr = cnt["sV"]

        for dd in range(DD):
            op("scalar", lambda e, dd=dd: e.activation(
                vtmp[:, dd, :], x_fm[:, dd, :], AF.Square),
                waits=((snap("sP", "sV") + [("sG", prev_st_g)]) if dd == 0
                       else [])
                + (p0c if j == 0 and dd == 0 else []))
        sq_done = cnt["sA"]
        for dd in range(DD):
            op("tensor", lambda e, dd=dd: e.matmul(
                psS2[:], onesr[:], x_fm[:, dd, :], start=(dd == 0), stop=(dd == DD - 1)),
                waits=[("sV", x_tr)] if dd == 0 else [])
        for dd in range(DD):
            op("tensor", lambda e, dd=dd: e.matmul(
                psS1[:], onesr[:], vtmp[:, dd, :], start=(dd == 0), stop=(dd == DD - 1)),
                waits=[("sA", sq_done)] if dd == 0 else [])
        ln_mm = cnt["sP"]

        def ln_stats(e):
            e.tensor_single_scalar(stt2, psS2[:], 1.0 / D, ALU.mult)
            e.tensor_single_scalar(stt1, psS1[:], 1.0 / D, ALU.mult)
            e.tensor_tensor(stt3, stt2, stt2, ALU.mult)
            return e.tensor_tensor(stt1, stt1, stt3, ALU.subtract)
        op("vector", ln_stats, waits=[("sP", ln_mm)])
        op("scalar", lambda e: e.activation(stt1, stt1, AF.Ln, bias=cbias[0:1, 0:1]),
           waits=snap("sV"))
        op("scalar", lambda e: e.activation(stt1, stt1, AF.Exp, scale=-0.5))
        op("vector", lambda e: e.tensor_tensor(stt2, stt2, stt1, ALU.mult),
           waits=snap("sA"))
        ln_st = cnt["sV"]
        op("tensor", lambda e: e.matmul(psB1[:], onesc[:], stt1, start=True,
                                        stop=True), waits=[("sV", ln_st)])
        op("tensor", lambda e: e.matmul(psB2[:], onesc[:], stt2, start=True,
                                        stop=True))
        ln_bc = cnt["sP"]
        op("vector", lambda e: e.tensor_tensor(
            vtmp[:, 0:DD, :], x_fm[:],
            psB1[:].unsqueeze(1).broadcast_to((128, DD, Tc)), ALU.mult),
            waits=[("sP", ln_bc)])
        op("vector", lambda e: e.tensor_tensor(
            vtmp[:, 0:DD, :], vtmp[:, 0:DD, :],
            psB2[:].unsqueeze(1).broadcast_to((128, DD, Tc)), ALU.subtract))
        op("scalar", lambda e: e.activation(
            flat(xn_bf), vt_f[:, 0:DD * Tc], AF.Identity), waits=snap("sV"))
        xn_done = cnt["sV"]
        xn_bf_done = cnt["sA"]
        for q2_ in range(2):
            op("vector", lambda e, q2_=q2_: fq_chain(
                e, vt_f[:, q2_ * 1024:(q2_ + 1) * 1024],
                flat(xnq)[:, q2_ * 1024:(q2_ + 1) * 1024], ci1[:, 0:1024],
                ci2[:, 0:1024], ci1[:, 0:1024],
                yt_f[:, 0:1024], yt_f[:, 1024:2048], ci1[:, 0:1024]))

        op("vector", lambda e: e.tensor_copy(xi[:, :, 0:3], halo[:]),
           waits=snap("sA", "sV"))
        ev_sA = [cnt["sA"], cnt["sA"]]
        for m in range(2 * DC):
            par = m % 2
            pst = psA0 if par == 0 else psA1
            if m == 2:
                wsem, wbuf, wwar = "wC", wb_out[:, 0:DD, :], wbo_last[0]
            elif m == 3:
                wsem, wbuf, wwar = "wD", wb_out[:, DD:2 * DD, :], wbo_last[0]
            elif par == 0:
                wsem, wbuf, wwar = "wA", wb_in[:, 0, :, :], wbin_last[0]
            else:
                wsem, wbuf, wwar = "wB", wb_in[:, 1, :, :], wbin_last[1]
            dma("sync", wbuf, winb_s[m],
                waits=[("sP", wwar)] + (p0w if j == 0 else []),
                sem=wsem)
            wv = cnt[wsem]
            for kc in range(DD):
                op("tensor", lambda e, kc=kc, wbuf=wbuf, pst=pst: e.matmul(
                    pst[:], wbuf[:, kc, :], xn_bf[:, kc, :],
                    start=(kc == 0), stop=(kc == DD - 1)),
                    waits=([(wsem, wv), ("sV", xn_done), ("sA", xn_bf_done),
                            ("sA", ev_sA[par])]) if kc == 0 else [])
            if m < DC:
                op("scalar", lambda e, m=m, pst=pst: e.activation(
                    xi[:, m, 3:3 + Tc], pst[:], AF.Identity), waits=snap("sP"))
                ev_sA[par] = cnt["sA"]

                def conv_mc(e, mc=m):
                    e.scalar_tensor_tensor(
                        q_y1, xi[:, mc, 0:Tc], wconv[:, mc, 0:1],
                        bconv[:, mc:mc + 1].broadcast_to((128, Tc)),
                        ALU.mult, ALU.add)
                    for k in range(1, 3):
                        e.scalar_tensor_tensor(
                            q_y1, xi[:, mc, k:k + Tc], wconv[:, mc, k:k + 1],
                            q_y1, ALU.mult, ALU.add)
                    return e.scalar_tensor_tensor(
                        q_y1, xi[:, mc, 3:3 + Tc], wconv[:, mc, 3:4],
                        q_y1, ALU.mult, ALU.add)
                op("vector", conv_mc, waits=snap("sA"))
                op("scalar", lambda e, mc=m: e.activation(
                    xc[:, mc, :], q_y1, AF.Silu), waits=snap("sV"))
            else:
                op("scalar", lambda e, m=m, pst=pst: e.activation(
                    zs[:, m - DC, :], pst[:], AF.Silu), waits=snap("sP"))
                ev_sA[par] = cnt["sA"]
            if m in (2, 3):
                wbo_last[0] = cnt["sP"]
            else:
                wbin_last[par] = cnt["sP"]
        wbo_last[1] = cnt["sP"]
        mm1_v = cnt["sV"]
        mm1_a = cnt["sA"]
        xc_a = cnt["sA"]
        op("vector", lambda e: e.tensor_copy(halo[:], xi[:, :, Tc:Tc + 3]),
           waits=snap("sV", "sA"))
        op("scalar", lambda e: e.activation(
            xcb_f[:, 0:DC * Tc], xc_f[:, 0:DC * Tc], AF.Identity),
           waits=snap("sV"))
        xcb_done = cnt["sA"]

        for kc in range(DC):
            op("tensor", lambda e, kc=kc: e.matmul(
                psA0[0:RDT, :], wx_sb[:, kc, 0:RDT], xcb[:, kc, :],
                start=(kc == 0), stop=(kc == DC - 1)),
                waits=([("sA", xcb_done)] + snap("sV")) if kc == 0 else [])
        for kc in range(DC):
            op("tensor", lambda e, kc=kc: e.matmul(
                psA1[0:NST, :], wx_sb[:, kc, RDT:RDT + NST], xcb[:, kc, :],
                start=(kc == 0), stop=(kc == DC - 1)))
        for kc in range(DC):
            op("tensor", lambda e, kc=kc: e.matmul(
                psT[0:NST, :], wx_sb[:, kc, RDT + NST:96], xcb[:, kc, :],
                start=(kc == 0), stop=(kc == DC - 1)))
        wx_done = cnt["sP"]
        op("vector", lambda e: e.tensor_copy(dtlow[:], psA0[0:RDT, :]),
           waits=[("sP", wx_done)] + snap("sG"))
        op("vector", lambda e: e.tensor_copy(bsb, psA1[0:NST, :]))
        op("vector", lambda e: e.tensor_copy(csb, psT[0:NST, :]))
        bc_ev = cnt["sV"]
        dma("gpsimd", b_sc[:, t0:t0 + Tc], bsb,
            waits=[("sV", bc_ev)] + (p0s if j == 0 else []))
        dma("gpsimd", c_sc[:, t0:t0 + Tc], csb)
        bc_out = cnt["sG"]
        dtev = [0, 0]
        for mc in range(DC):
            par = mc % 2
            pst = psA0 if par == 0 else psA1
            op("tensor", lambda e, mc=mc, pst=pst: e.matmul(
                pst[:], wdt_sb[:, mc, :], dtlow[:], start=True, stop=True),
                waits=([("sV", bc_ev)] if mc < 2
                       else (snap("sV") + [("sA", dtev[par])])))
            op("scalar", lambda e, mc=mc, pst=pst: e.activation(
                vtmp[:, mc, :], pst[:], AF.Identity, bias=bdt[:, mc:mc + 1]),
                waits=snap("sP"))
            dtev[par] = cnt["sA"]
        CW = DC * Tc
        op("vector", lambda e: e.tensor_single_scalar(
            xi[:, :, 0:Tc], vtmp[:], -1.0, ALU.mult),
            waits=[("sA", dtev[1])])
        op("vector", lambda e: e.tensor_tensor(
            yt[:], vtmp[:], xi[:, :, 0:Tc], ALU.min))
        sp_v = cnt["sV"]
        op("scalar", lambda e: e.activation(yt_f[:, 0:CW], yt_f[:, 0:CW], AF.Exp),
           waits=[("sV", sp_v)])
        op("scalar", lambda e: e.activation(yt_f[:, 0:CW], yt_f[:, 0:CW], AF.Ln,
                                            bias=cbias[:, 1:2]))
        sp_a = cnt["sA"]
        op("vector", lambda e: e.tensor_scalar(
            flat(dtt), vt_f[:, 0:CW], 0.0, 1e-4, ALU.max, ALU.add),
            waits=[("sG", bc_out)])
        op("vector", lambda e: e.tensor_tensor(
            flat(dtt), flat(dtt), yt_f[:, 0:CW], ALU.add), waits=[("sA", sp_a)])
        op("vector", lambda e: e.tensor_tensor(
            xi[:, :, 0:Tc], dtt[:], xc[:], ALU.mult))
        w_done = cnt["sV"]

        fb_prev = [w_done]

        def emit_ffn1_group(g):
            ps4 = (psA0, psA1, psB1, psB2)
            for i in range(4):
                m = 4 * g + i
                par = m % 2
                pst = ps4[i]
                wsem = "wE" if par == 0 else "wF"
                dma("sync", wb_1[:, par, :, :], w1q_s[m],
                    waits=[("sP", wb1_last[par])] + (p0w if j == 0 else []),
                    sem=wsem)
                wv = cnt[wsem]
                for kc in range(DD):
                    op("tensor", lambda e, kc=kc, par=par, pst=pst: e.matmul(
                        pst[:], wb_1[:, par, kc, :], xnq[:, kc, :],
                        start=(kc == 0), stop=(kc == DD - 1)),
                        waits=([(wsem, wv), ("sV", fb_prev[0])])
                        if kc == 0 else [])
                wb1_last[par] = cnt["sP"]
                op("vector", lambda e, i=i, m=m, pst=pst: e.tensor_scalar(
                    vtmp[:, i, :], pst[:], b1q[:, m:m + 1], None, ALU.add),
                    waits=snap("sP"))

            def ffn1b(e, g=g):
                e.tensor_single_scalar(qb_i1, qb_y1.bitcast(I32), 0x7F800000,
                                       ALU.bitwise_and)
                e.tensor_single_scalar(qb_i1, qb_i1, 40 << 23, ALU.max)
                e.tensor_single_scalar(qb_i2, qb_i1, 7 << 23, ALU.subtract)
                e.tensor_scalar(qb_i1, qb_i2, -1, 254 << 23, ALU.mult, ALU.add)
                e.tensor_tensor(qb_m, qb_y1, qb_i1.bitcast(F32), ALU.mult)
                e.tensor_single_scalar(qb_m, qb_m, 768.0, ALU.add)
                e.tensor_single_scalar(qb_i1, qb_m.bitcast(I32), ~0x3FFF,
                                       ALU.bitwise_and)
                e.tensor_tensor(qb_t2, qb_m, qb_i1.bitcast(F32), ALU.subtract)
                e.scalar_tensor_tensor(qb_st, qb_t2, 0.5, qb_i1.bitcast(F32),
                                       ALU.is_ge, ALU.add)
                e.tensor_scalar(qb_t2, qb_st, 768.0, -768.0, ALU.max, ALU.add)
                e.tensor_scalar(qb_i1, qb_i2, 16 << 23, 122 << 23,
                                ALU.add, ALU.min)
                e.tensor_tensor(qb_m, qb_t2, qb_i1.bitcast(F32), ALU.mult)
                e.tensor_single_scalar(qb_m, qb_m, 8.0, ALU.add)
                e.tensor_single_scalar(qb_i1, qb_m.bitcast(I32), ~0xFFFFF,
                                       ALU.bitwise_and)
                e.tensor_tensor(qb_t2, qb_m, qb_i1.bitcast(F32), ALU.subtract)
                e.scalar_tensor_tensor(qb_y1, qb_t2, 0.5, qb_i1.bitcast(F32),
                                       ALU.is_gt, ALU.add)
                e.tensor_single_scalar(qb_i1, qb_i1, 20, ALU.logical_shift_right)
                e.tensor_single_scalar(qb_i1, qb_i1, 1, ALU.bitwise_and)
                e.tensor_copy(qb_m, qb_i1)
                e.tensor_single_scalar(qb_t2, qb_t2, 0.5, ALU.is_equal)
                e.tensor_tensor(qb_t2, qb_t2, qb_m, ALU.mult)
                e.tensor_tensor(qb_y1, qb_y1, qb_t2, ALU.add)
                e.tensor_scalar(qb_i2, qb_i2, 5 << 23, 111 << 23,
                                ALU.add, ALU.max)
                return e.scalar_tensor_tensor(
                    hq[:, 4 * g:4 * g + 4, :].rearrange("p a b -> p (a b)"),
                    qb_y1, -8.0, qb_i2.bitcast(F32), ALU.add, ALU.mult)
            op("vector", ffn1b)
            fb_prev[0] = cnt["sV"]

        NK = 2 * DC
        tmp8 = xrow[:, 0, 0:NH]
        tw = [w_done, w_done]
        dbu_done_k = [0] * NK
        dma("gpsimd", brep[:],
            b_sc[0:NH, t0:t0 + Tc].unsqueeze(0).broadcast_to((128, NH, Tc)),
            waits=[("sG", bc_out)])
        dma("gpsimd", crep[:],
            c_sc[0:NH, t0:t0 + Tc].unsqueeze(0).broadcast_to((128, NH, Tc)))
        rep_ld = cnt["sG"]
        op("gpsimd", lambda e: e.tensor_tensor(
            sB[:], xi[:, 0, 0:Tc].unsqueeze(1).broadcast_to((128, NH, Tc)),
            brep[:], ALU.mult),
            waits=[("sG", rep_ld), ("sV", w_done)])
        dbu_done_k[0] = cnt["sG"]
        rep_b = rep_c = 0
        sc_prev = w_done
        for k in range(NK):
            ng, dc = divmod(k, DC)
            n0 = ng * NH
            sBb = sB if k % 2 == 0 else sB2
            sBn = sB2 if k % 2 == 0 else sB
            if k == DC - 1:
                dma("gpsimd", brep[:],
                    b_sc[NH:NST, t0:t0 + Tc].unsqueeze(0)
                    .broadcast_to((128, NH, Tc)), sem="wA")
                rep_b = cnt["wA"]
            if k == DC:
                dma("gpsimd", crep[:],
                    c_sc[NH:NST, t0:t0 + Tc].unsqueeze(0)
                    .broadcast_to((128, NH, Tc)), sem="wB",
                    waits=[("sV", tw[(k + 1) % 2])])
                rep_c = cnt["wB"]
            for n in range(NH):
                op("scalar", lambda e, dc=dc, n=n, n0=n0: e.activation(
                    sA[:, n, :], dtt[:, dc, :], AF.Exp,
                    scale=asb[:, dc * NST + n0 + n:dc * NST + n0 + n + 1]),
                    waits=[("sV", sc_prev)] if n == 0 else [])
            exp_done = cnt["sA"]
            if k + 1 < NK and k + 1 != DC:
                dcn = (k + 1) % DC
                op("gpsimd", lambda e, dcn=dcn, sBn=sBn: e.tensor_tensor(
                    sBn[:],
                    xi[:, dcn, 0:Tc].unsqueeze(1).broadcast_to((128, NH, Tc)),
                    brep[:], ALU.mult),
                    waits=[("sV", tw[(k + 1) % 2])])
                dbu_done_k[k + 1] = cnt["sG"]
            op("vector", lambda e, dc=dc, n0=n0: e.tensor_tensor(
                tmp8, sA[:, :, 0], hstate[:, dc, n0:n0 + NH], ALU.mult),
                waits=[("sA", exp_done), ("sG", dbu_done_k[k])] + snap("sV"))
            op("vector", lambda e, sBb=sBb: e.tensor_tensor(
                sBb[:, :, 0], sBb[:, :, 0], tmp8, ALU.add), waits=snap("sV"))
            op("vector", lambda e: e.memset(sA[:, :, 0:1], 0.0),
               waits=snap("sV"))
            op("vector", lambda e, sBb=sBb: e.tensor_tensor_scan(
                flat(sBb), flat(sA), flat(sBb), 0.0, ALU.mult, ALU.add),
                waits=snap("sV"))
            sc_prev = cnt["sV"]
            op("vector", lambda e, dc=dc, n0=n0, sBb=sBb: e.tensor_copy(
                hstate[:, dc, n0:n0 + NH], sBb[:, :, Tc - 1]),
                waits=snap("sV"))
            op("vector", lambda e, sBb=sBb: e.tensor_tensor(
                flat(sBb), flat(sBb), flat(crep), ALU.mult),
                waits=[("wB", rep_c)] if k == DC else [])
            op("vector", lambda e, sBb=sBb: e.tensor_tensor(
                sBb[:, 0:4, :].rearrange("p a b -> p (a b)"),
                sBb[:, 0:4, :].rearrange("p a b -> p (a b)"),
                sBb[:, 4:8, :].rearrange("p a b -> p (a b)"), ALU.add))
            op("vector", lambda e, sBb=sBb: e.tensor_tensor(
                sBb[:, 0:2, :].rearrange("p a b -> p (a b)"),
                sBb[:, 0:2, :].rearrange("p a b -> p (a b)"),
                sBb[:, 2:4, :].rearrange("p a b -> p (a b)"), ALU.add))
            if ng == 0:
                op("vector", lambda e, dc=dc, sBb=sBb: e.tensor_tensor(
                    yt[:, dc, :], sBb[:, 0, :], sBb[:, 1, :], ALU.add))
            else:
                op("vector", lambda e, dc=dc, sBb=sBb: e.tensor_tensor(
                    sBb[:, 0, :], sBb[:, 0, :], sBb[:, 1, :], ALU.add))
                op("vector", lambda e, dc=dc, sBb=sBb: e.tensor_tensor(
                    yt[:, dc, :], yt[:, dc, :], sBb[:, 0, :], ALU.add))
            tw[k % 2] = cnt["sV"]
            if k + 1 == DC:
                op("gpsimd", lambda e, sBn=sBn: e.tensor_tensor(
                    sBn[:],
                    xi[:, 0, 0:Tc].unsqueeze(1).broadcast_to((128, NH, Tc)),
                    brep[:], ALU.mult),
                    waits=[("wA", rep_b), ("sV", tw[(k + 1) % 2])])
                dbu_done_k[k + 1] = cnt["sG"]
            if k % 4 == 3:
                emit_ffn1_group(k // 4)
        ssm_done = cnt["sV"]
        hq_done = cnt["sV"]

        for dc in range(DC):
            op("vector", lambda e, dc=dc: e.scalar_tensor_tensor(
                yt[:, dc, :], xc[:, dc, :], dvec[:, dc:dc + 1], yt[:, dc, :],
                ALU.mult, ALU.add))
        op("vector", lambda e: e.tensor_tensor(
            yt_f[:, 0:CW], yt_f[:, 0:CW], flat(zs), ALU.mult))
        y_done = cnt["sV"]
        op("scalar", lambda e: e.activation(vt_f[:, 0:CW], yt_f[:, 0:CW], AF.Square),
           waits=[("sV", y_done)])
        y2_done = cnt["sA"]
        for dc in range(DC):
            op("tensor", lambda e, dc=dc: e.matmul(
                psS1[:], onesr[:], vtmp[:, dc, :], start=(dc == 0), stop=(dc == DC - 1)),
                waits=[("sA", y2_done)] if dc == 0 else [])
        rms_mm = cnt["sP"]
        op("vector", lambda e: e.tensor_single_scalar(stt1, psS1[:], 1.0 / E,
                                                      ALU.mult), waits=[("sP", rms_mm)])
        rv = cnt["sV"]
        op("scalar", lambda e: e.activation(stt1, stt1, AF.Ln, bias=cbias[0:1, 2:3]),
           waits=[("sV", rv)])
        op("scalar", lambda e: e.activation(stt1, stt1, AF.Exp, scale=-0.5))
        rsa = cnt["sA"]
        op("tensor", lambda e: e.matmul(psB1[:], onesc[:], stt1, start=True,
                                        stop=True), waits=[("sA", rsa)])
        rms_bc = cnt["sP"]
        prev_rms_bc = rms_bc
        for dc in range(DC):
            op("vector", lambda e, dc=dc: e.scalar_tensor_tensor(
                xcb[:, dc, :], yt[:, dc, :], rmsw[:, dc:dc + 1], psB1[:],
                ALU.mult, ALU.mult),
                waits=[("sP", rms_bc)] if dc == 0 else [])
        yrms_done = cnt["sV"]
        wbo_bufs = (wb_out[:], wb_in[:].rearrange("p a b c -> p (a b) c"))
        for md in range(DD):
            par = md % 2
            bufv = wbo_bufs[par]
            wsem = "wC" if par == 0 else "wD"
            dma("sync", bufv, woutb_s[md],
                waits=[("sP", wbo_last[par])] + (p0w if j == 0 else []),
                sem=wsem)
            wv = cnt[wsem]
            for kc in range(DC):
                op("tensor", lambda e, kc=kc, md=md, bufv=bufv: e.matmul(
                    psA0[:] if md % 2 == 0 else psA1[:],
                    bufv[:, kc, :], xcb[:, kc, :],
                    start=(kc == 0), stop=(kc == DC - 1)),
                    waits=([(wsem, wv), ("sV", yrms_done)] + snap("sV"))
                    if kc == 0 else [])
            wbo_last[par] = cnt["sP"]
            op("vector", lambda e, md=md: e.tensor_tensor(
                x_fm[:, md, :], x_fm[:, md, :],
                psA0[:] if md % 2 == 0 else psA1[:], ALU.add),
                waits=snap("sP"))
        wbin_last[0] = wbin_last[1] = cnt["sP"]
        attn_done = cnt["sV"]

        op("scalar", lambda e: e.activation(
            yt[:, 0:DD, :].rearrange("p a b -> p (a b)"), flat(mkb), AF.Identity),
           waits=[("wM", mk_ld)] + snap("sV"))
        mk_f32 = cnt["sA"]
        for md in range(DD):
            dma("sync", wb_2[:, 0:MC1 // 2, :], w2q_s[md, :, 0:MC1 // 2, :],
                waits=[("sP", wb2_last[0])] + (p0w if j == 0 else []),
                sem="wG")
            wv0 = cnt["wG"]
            dma("sync", wb_2[:, MC1 // 2:MC1, :], w2q_s[md, :, MC1 // 2:MC1, :],
                waits=[("sP", wb2_last[1])], sem="wH")
            wv1 = cnt["wH"]
            for kc in range(MC1):
                op("tensor", lambda e, kc=kc, md=md: e.matmul(
                    psA0[:] if md % 2 == 0 else psA1[:], wb_2[:, kc, :],
                    hq[:, kc, :],
                    start=(kc == 0), stop=(kc == MC1 - 1)),
                    waits=([("wG", wv0), ("sV", hq_done)] + snap("sV"))
                    if kc == 0 else ([("wH", wv1)] if kc == MC1 // 2 else []))
                if kc == MC1 // 2 - 1:
                    wb2_last[0] = cnt["sP"]
            wb2_last[1] = cnt["sP"]
            op("vector", lambda e, md=md: e.tensor_scalar(
                vtmp[:, md % 4, :],
                (psA0 if md % 2 == 0 else psA1)[:],
                b2q[:, md:md + 1], None, ALU.add),
                waits=snap("sP"))

            if md % 4 == 3:
                g = md // 4

                def ffn2b(e, g=g):
                    e.tensor_single_scalar(qb_i1, qb_y1.bitcast(I32),
                                           0x7F800000, ALU.bitwise_and)
                    e.tensor_single_scalar(qb_i1, qb_i1, 40 << 23, ALU.max)
                    e.tensor_single_scalar(qb_i2, qb_i1, 7 << 23, ALU.subtract)
                    e.tensor_scalar(qb_i1, qb_i2, -1, 254 << 23,
                                    ALU.mult, ALU.add)
                    e.tensor_tensor(qb_m, qb_y1, qb_i1.bitcast(F32), ALU.mult)
                    e.tensor_single_scalar(qb_m, qb_m, 768.0, ALU.add)
                    e.tensor_single_scalar(qb_i1, qb_m.bitcast(I32), ~0x3FFF,
                                           ALU.bitwise_and)
                    e.tensor_tensor(qb_t2, qb_m, qb_i1.bitcast(F32),
                                    ALU.subtract)
                    e.scalar_tensor_tensor(qb_st, qb_t2, 0.5,
                                           qb_i1.bitcast(F32),
                                           ALU.is_ge, ALU.add)
                    e.scalar_tensor_tensor(qb_t2, qb_st, -768.0,
                                           qb_i2.bitcast(F32),
                                           ALU.add, ALU.mult)
                    e.tensor_tensor(
                        qb_t2, qb_t2,
                        yt[:, 4 * g:4 * g + 4, :].rearrange("p a b -> p (a b)"),
                        ALU.mult)
                    return e.tensor_tensor(
                        x_fm[:, 4 * g:4 * g + 4, :].rearrange("p a b -> p (a b)"),
                        x_fm[:, 4 * g:4 * g + 4, :].rearrange("p a b -> p (a b)"),
                        qb_t2, ALU.add)
                op("vector", ffn2b,
                   waits=[("sA", mk_f32)] if md == 3 else [])
        ffn_done = cnt["sV"]

        st12 = vtmp[:, 0:4, :].rearrange("p a b -> p (a b)").rearrange(
            "p (a b) -> p a b", b=128)
        for ts in range(2):
            for dd in range(DD):
                op("tensor", lambda e, ts=ts, dd=dd: e.transpose(
                    psT[:, 0:128], x_fm[:, dd, ts * 128:(ts + 1) * 128], ident[:]),
                    waits=[("sV", ffn_done)] + snap("sV", "sG", "sA"))
                op("vector", lambda e, dd=dd: e.tensor_copy(
                    st12[:, dd, :], psT[:, 0:128]), waits=snap("sP"))
            op("gpsimd", lambda e, ts=ts, j=j: e.dma_start(
                out=y_dram[j, ts], in_=st12[:]), waits=snap("sV"), d=16)
        prev_st_g = cnt["sG"]

    if dbg == 2:
        o = nc.dram_tensor("d_dummy", [128, NST], F32, kind="ExternalOutput").ap()
        dma("gpsimd", o, hstate[:, 0, :], waits=snap("sV", "sA", "sP", "sS"))
    if dbg is True:
        dbg_tiles = {
            "d_xfm": (x_fm, F32), "d_xnbf": (xn_bf, BF16), "d_xnq": (xnq, FP8),
            "d_xi": (xi, F32), "d_xc": (xc, F32), "d_zs": (zs, F32),
            "d_dtt": (dtt, F32), "d_yt": (yt, F32), "d_xcb": (xcb, BF16),
            "d_hstate": (hstate, F32), "d_dtlow": (dtlow, BF16),
            "d_brep": (brep, F32),
            "d_crep": (crep, F32), "d_sA": (sA, F32), "d_sB": (sB, F32),
            "d_vtmp": (vtmp, F32), "d_b1q": (b1q, F32), "d_asb": (asb, F32),
            "d_hq": (hq, FP8),
        }
        for name, (tile, dt_) in dbg_tiles.items():
            o = nc.dram_tensor(name, list(tile[:].shape), dt_, kind="ExternalOutput").ap()
            dma("gpsimd", o, tile[:], waits=snap("sV", "sA", "sP", "sS"))
    final_g = cnt["sG"]

    with nc.Block() as block:
        def run_engine(ename):
            def body(eng):
                for (tag, waits, fn, inc) in sched:
                    if tag != ename:
                        continue
                    for (s, v) in waits:
                        eng.wait_ge(sems[s], v)
                    inst = fn(eng)
                    if inc is not None:
                        inst.then_inc(sems[inc[0]], inc[1])
                if ename == "gpsimd":
                    eng.wait_ge(sems["sG"], final_g)
            return body
        block.sync(run_engine("sync"))
        block.gpsimd(run_engine("gpsimd"))
        block.tensor(run_engine("tensor"))
        block.vector(run_engine("vector"))
        block.scalar(run_engine("scalar"))
    return nc


def _host_inputs(inputs):
    f32 = np.float32
    x = np.ascontiguousarray(np.asarray(inputs["L0_attn_x"], f32))
    W_in = np.asarray(inputs["L0_attn_W_in"], f32)
    W_conv = np.asarray(inputs["L0_attn_W_conv"], f32)
    b_conv = np.asarray(inputs["L0_attn_b_conv"], f32)
    W_x = np.asarray(inputs["L0_attn_W_x"], f32)
    W_dt = np.asarray(inputs["L0_attn_W_dt"], f32)
    b_dt = np.asarray(inputs["L0_attn_b_dt"], f32)
    A_log = np.asarray(inputs["L0_attn_A_log"], f32)
    Dv = np.asarray(inputs["L0_attn_D"], f32)
    rms_w = np.asarray(inputs["L0_attn_rms_w"], f32)
    W_out = np.asarray(inputs["L0_attn_W_out"], f32)
    W1 = np.asarray(inputs["L0_ffn_W1"], f32)
    b1 = np.asarray(inputs["L0_ffn_b1"], f32)
    W2 = np.asarray(inputs["L0_ffn_W2"], f32)
    b2 = np.asarray(inputs["L0_ffn_b2"], f32)

    import os
    import jax
    with jax.default_device(jax.devices("cpu")[0]):
        want_x64 = os.environ.get("BASS_SIM_FP64") == "1"
        had_x64 = jax.config.jax_enable_x64
        if want_x64 and not had_x64:
            jax.config.update("jax_enable_x64", True)
        try:
            keep = np.asarray(jax.random.bernoulli(
                jax.random.key(42, impl="threefry2x32"), 0.8, (8, T, D)))
        finally:
            if want_x64 and not had_x64:
                jax.config.update("jax_enable_x64", False)
    maskv = np.where(keep, np.float32(1.25), np.float32(0.0)).astype(ml_dtypes.bfloat16)

    def pd(v, chunks):
        return np.ascontiguousarray(np.reshape(v, (chunks, 128)).T)

    def tile4(wT, KB, MB):
        return np.ascontiguousarray(
            wT.reshape(KB, 128, MB, 128).transpose(2, 1, 0, 3))

    shared = {
        "w_inT": tile4(W_in.T, DD, 2 * E // 128),
        "w_xT": np.ascontiguousarray(W_x.T),
        "w_dtT": np.ascontiguousarray(W_dt.T),
        "w_outT": tile4(W_out.T, DC, DD),
        "w1T": tile4(W1.T, DD, MC1),
        "w2T": tile4(W2.T, MC1, DD),
        "bconv_in": pd(b_conv, DC),
        "wconv_in": np.ascontiguousarray(W_conv.reshape(DC, 128, 4).transpose(1, 0, 2)),
        "bdt_in": pd(b_dt, DC),
        "alog_in": np.ascontiguousarray(
            A_log.reshape(DC, 128, NST).transpose(1, 0, 2).reshape(128, DC * NST)),
        "dvec_in": pd(Dv, DC),
        "rmsw_in": pd(rms_w, DC),
        "b1_in": pd(b1, MC1),
        "b2_in": pd(b2, DD),
        "ident_in": np.eye(128, dtype=f32),
        "onesc_in": np.ones((1, 128), f32),
        "onesr_in": np.ones((128, 1), f32),
    }
    maps = []
    for c in range(8):
        m = dict(shared)
        m["x_in"] = x[c]
        m["mask_in"] = np.ascontiguousarray(maskv[c].T)
        maps.append(m)
    return maps


def kernel(**inputs):
    maps = _host_inputs(inputs)
    nc = build_core()
    res = run_bass_kernel_spmd(nc, maps, core_ids=list(range(8)))
    out = np.stack([np.asarray(r["y_out"]) for r in res.results], axis=0)
    return np.ascontiguousarray(out.astype(np.float32))
